# version 01
# baseline (speedup 1.0000x reference)
import numpy as np
import ml_dtypes
from contextlib import ExitStack

import concourse.bass as bass
import concourse.mybir as mybir
from concourse.bass_utils import run_bass_kernel_spmd

F32 = mybir.dt.float32
BF16 = mybir.dt.bfloat16
AF = mybir.ActivationFunctionType
ALU = mybir.AluOpType
AX = mybir.AxisListType
BF = ml_dtypes.bfloat16

D = 2048
T = 2048
NE = 16
H = 8192
NJ = 64
THRC = float(0.5 + 2.0 ** -10)
LN_EPS = 1e-5
RMS_EPS = float(np.finfo(np.float32).eps)
EXP_BIAS = -15.0

PHASES = ("ffn1", "attn", "ffn2")
SUB = {"ln": True, "apply": True, "gu": True, "proj": True, "nj": NJ, "silu": True, "up": True, "hmul": True, "sq": True, "stats": True, "hdma": True, "rms": True, "silu_fn": "Silu", "noquant": False}

ENGS = ("sp", "pe", "act", "dve", "po")
MS = {"sp": "msp", "pe": "mpe", "act": "mact", "dve": "mdve", "po": "mpo"}
SEMS = ("msp", "mpe", "mact", "mdve", "mpo")


class Buf:
    __slots__ = ("w", "r")

    def __init__(self):
        self.w = None
        self.r = {}


class Prog:
    def __init__(self):
        self.streams = {e: [] for e in ENGS}
        self.cnt = {s: 0 for s in SEMS}
        self.dsems = []
        self.ewait = {e: {} for e in ENGS}
        self.bufs = {}
        self.pend = {e: [] for e in ENGS}

    def dsem(self, name):
        if name not in self.cnt:
            self.cnt[name] = 0
            self.dsems.append(name)
        return name

    def fence_now(self):
        f = dict(getattr(self, "fence", ()))
        for b in self.bufs.values():
            if b.w:
                s, v = b.w
                f[s] = max(f.get(s, 0), v)
            for s, v in b.r.items():
                f[s] = max(f.get(s, 0), v)
        self.fence = tuple(f.items())

    def buf(self, name):
        b = self.bufs.get(name)
        if b is None:
            b = self.bufs[name] = Buf()
        return b

    def emit(self, eng, fn, waits, inc, n):
        wl = {}
        for s, v in waits:
            if v > 0:
                wl[s] = max(wl.get(s, 0), v)
        eh = self.ewait[eng]
        final = []
        for s, v in wl.items():
            if v > eh.get(s, 0):
                assert v <= self.cnt[s], \
                    f"{eng}: wait {s}>={v} unreachable (emitted {self.cnt[s]})"
                final.append((s, v))
                eh[s] = v
        after = None
        if inc is not None:
            self.cnt[inc] += n
            after = self.cnt[inc]
        self.streams[eng].append((fn, tuple(final), inc, n))
        return after

    def op(self, eng, fn, r=(), w=(), inc="default", n=1):
        waits = []
        for name in r:
            b = self.buf(name)
            if b.w:
                waits.append(b.w)
        for name in w:
            b = self.buf(name)
            if b.w:
                waits.append(b.w)
            waits.extend(b.r.items())
            if b.w is None and not b.r:
                waits.extend(getattr(self, "fence", ()))
        if inc == "default":
            inc = MS[eng]
        after = self.emit(eng, fn, waits, inc, n)
        if inc is None:
            self.pend[eng].append((tuple(r), tuple(w)))
        else:
            for (pr, pw) in self.pend[eng] + [(tuple(r), tuple(w))]:
                for name in pr:
                    b = self.buf(name)
                    b.r[inc] = max(b.r.get(inc, 0), after)
                for name in pw:
                    b = self.buf(name)
                    b.w = (inc, after)
                    b.r = {}
            self.pend[eng] = []
        return after

    def dma(self, q, dst, src, r=(), w=(), sem=None):
        if sem is None:
            assert w, "dma needs w= slot or explicit sem"
            sem = "d_" + w[0]
        return self.op(q, lambda e, a=dst, b=src: e.dma_start(out=a, in_=b),
                       r=r, w=w, inc=self.dsem(sem), n=16)

    def barrier(self):
        for eng in ENGS:
            assert not self.pend[eng], f"pending notes on {eng} at barrier"
        waits = [(MS[o], self.cnt[MS[o]]) for o in ENGS]
        waits += [(s, self.cnt[s]) for s in self.dsems]
        for eng in ENGS:
            self.emit(eng, lambda e: e.nop(),
                      [wv for wv in waits if wv[0] != MS[eng]], None, 1)
        for b in self.bufs.values():
            b.w = None
            b.r = {}
        self.fence = ()


def _consts_np():
    ind16 = np.zeros((128, 512), dtype=BF)
    sel16 = np.zeros((32, 2048), dtype=BF)
    for ec in range(16):
        ind16[0:64, ec * 32 + 2 * ec] = 1
        ind16[64:128, ec * 32 + 2 * ec + 1] = 1
        sel16[2 * ec, ec * 128: ec * 128 + 64] = 1
        sel16[2 * ec + 1, ec * 128 + 64: ec * 128 + 128] = 1
    eye = np.eye(128, dtype=np.float32)
    return ind16, sel16, eye


def build(phases=None):
    if phases is None:
        phases = PHASES
    want_ffn = ("ffn1" in phases) or ("ffn2" in phases) or ("quant" in phases)
    want_attn = "attn" in phases

    nc = bass.Bass("TRN2", target_bir_lowering=False)
    p = Prog()

    xT = nc.dram_tensor("xT", [D, T], F32, kind="ExternalInput").ap()
    yT = nc.dram_tensor("yT", [D, T], F32, kind="ExternalOutput").ap()
    wq_d = nc.dram_tensor("wq", [D, D], F32, kind="ExternalInput").ap()
    wk_d = nc.dram_tensor("wk", [D, D], F32, kind="ExternalInput").ap()
    wv_d = nc.dram_tensor("wv", [D, D], F32, kind="ExternalInput").ap()
    wo_d = nc.dram_tensor("wo", [D, D], F32, kind="ExternalInput").ap()
    wguT = nc.dram_tensor("wguT", [D, 2 * H], F32, kind="ExternalInput").ap()
    wpT = nc.dram_tensor("wpT", [H, D], F32, kind="ExternalInput").ap()
    eye_d = nc.dram_tensor("eye_f", [128, 128], F32, kind="ExternalInput").ap()
    ind16_d = nc.dram_tensor("ind16", [128, 512], BF16, kind="ExternalInput").ap()
    sel16_d = nc.dram_tensor("sel16", [32, 2048], BF16, kind="ExternalInput").ap()

    s_wgu = nc.dram_tensor("s_wgu", [128 * 128, 2048], BF16, kind="Internal").ap()
    s_wp = nc.dram_tensor("s_wp", [128 * 128, 1024], BF16, kind="Internal").ap()
    s_h = nc.dram_tensor("s_h", [H, T], BF16, kind="Internal").ap()
    s_wq = nc.dram_tensor("s_wq", [D, D], BF16, kind="Internal").ap()
    s_wk = nc.dram_tensor("s_wk", [D, D], BF16, kind="Internal").ap()
    s_wv2 = nc.dram_tensor("s_wv2", [D, D], BF16, kind="Internal").ap()
    s_wo2 = nc.dram_tensor("s_wo2", [D, D], BF16, kind="Internal").ap()
    s_q = nc.dram_tensor("s_q", [16 * 16 * 128, 128], BF16, kind="Internal").ap()
    s_k = nc.dram_tensor("s_k", [T, D], BF16, kind="Internal").ap()
    s_v = nc.dram_tensor("s_v", [D, T], BF16, kind="Internal").ap()
    s_at = nc.dram_tensor("s_at", [16 * 16 * 128, 128], BF16, kind="Internal").ap()

    es = ExitStack()

    def sb(name, shape, dt):
        return es.enter_context(nc.sbuf_tensor(name, shape, dt))

    def ps(name, shape, dt):
        return es.enter_context(nc.psum_tensor(name, shape, dt))

    X = [sb(f"X{i}", [128, T], F32) for i in range(NE)]
    onesK1 = sb("onesK1", [1, 128], F32)
    ones05 = sb("ones05", [1, 128], F32)
    ones_bf = sb("ones_bf", [128, 1], BF16)
    eye_sb = sb("eye_sb", [128, 128], F32)
    cb = sb("cb", [128, 3], F32)

    P = [ps(f"P{i}", [128, 512], F32) for i in range(8)]

    p.op("dve", lambda e: e.memset(onesK1[:, :], 1.0), w=("onesK1",))
    p.op("dve", lambda e: e.memset(ones05[:, :], 0.5), w=("ones05",))
    p.op("dve", lambda e: e.memset(ones_bf[:, :], 1.0), w=("ones_bf",))
    p.op("dve", lambda e: e.memset(cb[:, 0:1], LN_EPS), w=("cb",))
    p.op("dve", lambda e: e.memset(cb[:, 1:2], RMS_EPS), w=("cb",))
    p.op("dve", lambda e: e.memset(cb[:, 2:3], EXP_BIAS), w=("cb",))
    p.dma("sp", eye_sb[:, :], eye_d, w=("eye_sb",))

    for i in range(NE):
        p.dma("sp", X[i][:, :], xT[i * 128:(i + 1) * 128, :], w=(f"X{i}",),
              sem="d_xload")
    for i in range(NE):
        p.bufs[f"X{i}"].w = ("d_xload", p.cnt["d_xload"])

    if want_attn:
        p.dma("po", s_wq, wq_d, sem="d_cast")
        p.dma("po", s_wk, wk_d, sem="d_cast")
        wv_r = wv_d.rearrange("(ec pp) (fo c) -> fo pp ec c", pp=128, c=128)
        wo_r = wo_d.rearrange("(ec pp) (fo c) -> fo pp ec c", pp=128, c=128)
        sv_r = s_wv2.rearrange("(fo pp) (ec c) -> fo pp ec c", pp=128, c=128)
        so_r = s_wo2.rearrange("(fo pp) (ec c) -> fo pp ec c", pp=128, c=128)
        for fo in range(16):
            p.dma("po", sv_r[fo], wv_r[fo], sem="d_cast")
            p.dma("po", so_r[fo], wo_r[fo], sem="d_cast")
        for nmx in ("s_wq", "s_wk", "s_wv2", "s_wo2"):
            p.buf(nmx).w = ("d_cast", p.cnt["d_cast"])

    if want_ffn and not SUB["noquant"]:
        with ExitStack() as qx:
            qsb = lambda nm, sh, dt: qx.enter_context(nc.sbuf_tensor(nm, sh, dt))
            ind16 = qsb("ind16s", [128, 512], BF16)
            sel16 = qsb("sel16s", [32, 2048], BF16)
            wt = [qsb(f"q_wt{i}", [128, 1024], BF16) for i in range(16)]
            aw = [qsb(f"q_aw{i}", [128, 1024], BF16) for i in range(2)]
            s_bf = qsb("q_sbf", [32, 1024], BF16)
            S_sb = [qsb(f"q_Ssb{i}", [128, 1024], BF16) for i in range(2)]
            m1 = [qsb(f"q_m1{i}", [128, 1024], BF16) for i in range(2)]
            m2 = [qsb(f"q_m2{i}", [128, 1024], BF16) for i in range(2)]
            wh = [qsb(f"q_wh{i}", [128, 1024], BF16) for i in range(2)]

            p.dma("sp", ind16[:, :], ind16_d, w=("ind16",))
            p.dma("sp", sel16[:, :], sel16_d, w=("sel16",))

            jobs = [("gu", st) for st in range(16)] + [("wp", st) for st in range(8)]
            for kind, st in jobs:
                for ec in range(16):
                    wslot = f"q_wt{ec}"
                    if kind == "gu":
                        src = wguT[ec * 128:(ec + 1) * 128, st * 1024:(st + 1) * 1024]
                    else:
                        hg, dgq = st // 2, st % 2
                        hq = hg * 16 + ec
                        src = wpT[hq * 128:(hq + 1) * 128, dgq * 1024:(dgq + 1) * 1024]
                    p.dma("po", wt[ec][:, :], src, w=(wslot,))
                    p.op("act", lambda e, ec=ec:
                         e.activation(out=aw[ec % 2][:, :], in_=wt[ec][:, :],
                                      func=AF.Abs),
                         r=(wslot,), w=(f"q_aw{ec % 2}",))
                    for nn in range(2):
                        p.op("pe", lambda e, ec=ec, nn=nn:
                             e.matmul(P[nn][0:32, :], ind16[:, ec * 32:(ec + 1) * 32],
                                      aw[ec % 2][:, nn * 512:(nn + 1) * 512],
                                      start=(ec == 0), stop=(ec == 15)),
                             r=(f"q_aw{ec % 2}", "ind16"),
                             w=((f"P{nn}",) if ec in (0, 15) else ()),
                             inc=(MS["pe"] if nn == 1 else None))
                for nn in range(2):
                    p.op("dve", lambda e, nn=nn:
                         e.tensor_scalar(out=s_bf[:, nn * 512:(nn + 1) * 512],
                                         in0=P[nn][0:32, :], scalar1=1.0 / 64,
                                         scalar2=1e-8, op0=ALU.mult, op1=ALU.max),
                         r=(f"P{nn}",), w=("q_sbf",))
                for ec in range(16):
                    sslot, mslot, nslot, whs = (f"q_Ssb{ec % 2}", f"q_m1{ec % 2}",
                                                f"q_m2{ec % 2}", f"q_wh{ec % 2}")
                    p.op("pe", lambda e, ec=ec:
                         e.matmul(P[2 + ec % 2][:, :],
                                  sel16[:, ec * 128:(ec + 1) * 128],
                                  s_bf[:, 0:512], start=True, stop=True),
                         r=("q_sbf", "sel16"), w=(f"P{2 + ec % 2}",))
                    p.op("act", lambda e, ec=ec:
                         e.activation(out=S_sb[ec % 2][:, 0:512],
                                      in_=P[2 + ec % 2][:, :], func=AF.Copy),
                         r=(f"P{2 + ec % 2}",), w=(sslot,))
                    p.op("pe", lambda e, ec=ec:
                         e.matmul(P[4 + ec % 2][:, :],
                                  sel16[:, ec * 128:(ec + 1) * 128],
                                  s_bf[:, 512:1024], start=True, stop=True),
                         r=("q_sbf", "sel16"), w=(f"P{4 + ec % 2}",))
                    p.op("act", lambda e, ec=ec:
                         e.activation(out=S_sb[ec % 2][:, 512:1024],
                                      in_=P[4 + ec % 2][:, :], func=AF.Copy),
                         r=(f"P{4 + ec % 2}",), w=(sslot,))
                    p.op("dve", lambda e, ec=ec:
                         e.scalar_tensor_tensor(out=m1[ec % 2][:, :],
                                                in0=S_sb[ec % 2][:, :], scalar=THRC,
                                                in1=wt[ec][:, :], op0=ALU.mult,
                                                op1=ALU.is_lt),
                         r=(sslot, f"q_wt{ec}"), w=(mslot,))
                    p.op("dve", lambda e, ec=ec:
                         e.scalar_tensor_tensor(out=m2[ec % 2][:, :],
                                                in0=S_sb[ec % 2][:, :], scalar=-THRC,
                                                in1=wt[ec][:, :], op0=ALU.mult,
                                                op1=ALU.is_gt),
                         r=(sslot, f"q_wt{ec}"), w=(nslot,))
                    p.op("po", lambda e, ec=ec:
                         e.tensor_tensor(out=m1[ec % 2][:, :], in0=m1[ec % 2][:, :],
                                         in1=m2[ec % 2][:, :], op=ALU.subtract),
                         r=(nslot,), w=(mslot,))
                    p.op("po", lambda e, ec=ec:
                         e.tensor_tensor(out=wh[ec % 2][:, :], in0=m1[ec % 2][:, :],
                                         in1=S_sb[ec % 2][:, :], op=ALU.mult),
                         r=(mslot, sslot), w=(whs,))
                    if kind == "gu":
                        j0 = st * 8
                        dst = s_wgu[j0 * 128:(j0 + 8) * 128,
                                    ec * 128:(ec + 1) * 128].rearrange(
                                        "(j pp) c -> pp j c", pp=128)
                        srcap = wh[ec % 2][:, :].rearrange("pp (j c) -> pp j c", j=8)
                        p.dma("po", dst, srcap, r=(whs,), w=("s_wgu",),
                              sem="d_" + whs)
                    else:
                        hg, dgq = st // 2, st % 2
                        hq = hg * 16 + ec
                        row0 = (dgq * 64 + hq) * 128
                        p.dma("po", s_wp[row0:row0 + 128, :], wh[ec % 2][:, :],
                              r=(whs,), w=("s_wp",), sem="d_" + whs)
        p.barrier()

    def ln_rows(r_mu, r_A, tag):
        with ExitStack() as lx:
            lsb = lambda nm, sh, dt: lx.enter_context(nc.sbuf_tensor(nm, sh, dt))
            xbf = [lsb(f"xbf{i}_{tag}", [128, T], BF16) for i in range(2)]
            sq = [lsb(f"sq{i}_{tag}", [128, T], BF16) for i in range(2)]
            for ec in range(NE):
                xs, qs = f"lnx{ec % 2}", f"lnq{ec % 2}"
                p.op("act", lambda e, ec=ec: e.activation(
                    out=xbf[ec % 2][:, :], in_=X[ec][:, :], func=AF.Copy),
                    r=(f"X{ec}",), w=(xs,))
                p.op("act", lambda e, ec=ec: e.activation(
                    out=sq[ec % 2][:, :], in_=xbf[ec % 2][:, :], func=AF.Square),
                    r=(xs,), w=(qs,))
                for nn in range(4):
                    p.op("pe", lambda e, ec=ec, nn=nn:
                         e.matmul(P[4 + nn][0:1, :], ones_bf[:, :],
                                  xbf[ec % 2][:, nn * 512:(nn + 1) * 512],
                                  start=(ec == 0), stop=(ec == 15)),
                         r=(xs, "ones_bf"),
                         w=((f"P{4 + nn}",) if ec in (0, 15) else ()),
                         inc=None)
                for nn in range(4):
                    p.op("pe", lambda e, ec=ec, nn=nn:
                         e.matmul(P[nn][0:1, :], ones_bf[:, :],
                                  sq[ec % 2][:, nn * 512:(nn + 1) * 512],
                                  start=(ec == 0), stop=(ec == 15)),
                         r=(qs, "ones_bf"),
                         w=((f"P{nn}",) if ec in (0, 15) else ()),
                         inc=(MS["pe"] if nn == 3 else None))
        p.fence_now()
        for nn in range(4):
            p.op("act", lambda e, nn=nn:
                 e.activation(out=r_mu[0:1, nn * 512:(nn + 1) * 512],
                              in_=P[4 + nn][0:1, :], func=AF.Copy, scale=1.0 / D),
                 r=(f"P{4 + nn}",), w=("r_mu",))
        p.op("dve", lambda e:
             e.tensor_tensor(out=r_A[0:1, :], in0=r_mu[0:1, :], in1=r_mu[0:1, :],
                             op=ALU.mult),
             r=("r_mu",), w=("r_A",))
        for nn in range(4):
            p.op("dve", lambda e, nn=nn:
                 e.scalar_tensor_tensor(out=r_A[0:1, nn * 512:(nn + 1) * 512],
                                        in0=P[nn][0:1, :], scalar=1.0 / D,
                                        in1=r_A[0:1, nn * 512:(nn + 1) * 512],
                                        op0=ALU.mult, op1=ALU.subtract),
                 r=(f"P{nn}",), w=("r_A",))
        for nn in range(4):
            p.op("act", lambda e, nn=nn:
                 e.activation(out=P[nn][0:1, :],
                              in_=r_A[0:1, nn * 512:(nn + 1) * 512],
                              func=AF.Sqrt, bias=cb[0:1, 0:1]),
                 r=("r_A", "cb"), w=(f"P{nn}",))
        for nn in range(4):
            p.op("dve", lambda e, nn=nn:
                 e.reciprocal(out=r_A[0:1, nn * 512:(nn + 1) * 512],
                              in_=P[nn][0:1, :]),
                 r=(f"P{nn}",), w=("r_A",))

    def ln_apply(r_mu, r_A, xn, Msb, Rsb, ts):
        for hf in range(2):
            base = ts * 1024 + hf * 512
            p.op("pe", lambda e, base=base:
                 e.matmul(P[6][:, :], onesK1[:, :], r_mu[0:1, base:base + 512],
                          start=True, stop=True),
                 r=("r_mu", "onesK1"), w=("P6",))
            p.op("pe", lambda e, base=base:
                 e.matmul(P[7][:, :], onesK1[:, :], r_A[0:1, base:base + 512],
                          start=True, stop=True),
                 r=("r_A", "onesK1"), w=("P7",))
            p.op("act", lambda e:
                 e.activation(out=Msb[:, :], in_=P[6][:, :], func=AF.Copy),
                 r=("P6",), w=("Msb",))
            p.op("act", lambda e:
                 e.activation(out=Rsb[:, :], in_=P[7][:, :], func=AF.Copy),
                 r=("P7",), w=("Rsb",))
            for ec in range(NE):
                sl = slice(hf * 512, hf * 512 + 512)
                p.op("dve", lambda e, ec=ec, sl=sl, base=base:
                     e.scalar_tensor_tensor(out=xn[ec][:, sl],
                                            in0=X[ec][:, base:base + 512], scalar=1.0,
                                            in1=Msb[:, :], op0=ALU.mult,
                                            op1=ALU.subtract),
                     r=(f"X{ec}", "Msb"), w=(f"xn{ec}",))
                p.op("dve", lambda e, ec=ec, sl=sl:
                     e.tensor_tensor(out=xn[ec][:, sl], in0=xn[ec][:, sl],
                                     in1=Rsb[:, :], op=ALU.mult),
                     r=("Rsb",), w=(f"xn{ec}",))

    def ffn_pass(tag):
        with ExitStack() as fx:
            fsb = lambda nm, sh, dt: fx.enter_context(
                nc.sbuf_tensor(nm + "_" + tag, sh, dt))
            r_mu = fsb("r_mu", [1, T], F32)
            r_A = fsb("r_A", [1, T], F32)
            r_S = fsb("r_S", [1, 1024], F32)
            Msb = fsb("Msb", [128, 512], F32)
            Rsb = fsb("Rsb", [128, 512], F32)
            xn = [fsb(f"xn{i}", [128, 1024], BF16) for i in range(NE)]

            if SUB["ln"]:
                ln_rows(r_mu, r_A, tag)
            Wg = [fsb(f"Wg{i}", [128, 2048], BF16) for i in range(2)]
            Wu = [fsb(f"Wu{i}", [128, 2048], BF16) for i in range(2)]
            sgb = fsb("sgb", [128, 1024], BF16)
            hst = fsb("hst", [128, 1024], BF16)
            h2b = fsb("h2b", [128, 1024], BF16)
            Wp = [Wg[0][:, 0:1024], Wg[1][:, 0:1024]]
            wpk = ("Wg0", "Wg1")
            hrd = [Wu[0][:, 0:512], Wu[1][:, 0:512]]
            hrk = ("Wu0", "Wu1")
            R_sb = sgb[:, :].bitcast(F32)
            prj = hst[:, :].bitcast(F32)
            for ts in range(2):
                if not SUB["apply"]:
                    continue
                if SUB["ln"]:
                    ln_apply(r_mu, r_A, xn, Msb, Rsb, ts)
                if not SUB["gu"]:
                    continue
                if True:
                    for j in range(SUB["nj"]):
                        gslot, uslot = f"Wg{j % 2}", f"Wu{j % 2}"
                        pgn = ("P0", "P1") if j % 2 == 0 else ("P2", "P3")
                        pg = (P[0], P[1]) if j % 2 == 0 else (P[2], P[3])
                        p.dma("sp", Wg[j % 2][:, :],
                              s_wgu[j * 128:(j + 1) * 128, :], w=(gslot,))
                        p.dma("sp", Wu[j % 2][:, :],
                              s_wgu[(64 + j) * 128:(64 + j + 1) * 128, :], w=(uslot,))
                        for nn in range(2):
                            for ec in range(NE):
                                p.op("pe", lambda e, nn=nn, ec=ec, pg=pg, j=j:
                                     e.matmul(pg[nn][:, :],
                                              Wg[j % 2][:, ec * 128:(ec + 1) * 128],
                                              xn[ec][:, nn * 512:(nn + 1) * 512],
                                              start=(ec == 0), stop=(ec == 15)),
                                     r=(gslot, f"xn{ec}"),
                                     w=((pgn[nn],) if ec in (0, 15) else ()),
                                     inc=(MS["pe"] if ec == 15 else None))
                        if SUB["silu"]:
                          for nn in range(2):
                            p.op("act", lambda e, nn=nn, pg=pg:
                                 e.activation(out=sgb[:, nn * 512:(nn + 1) * 512],
                                              in_=pg[nn][:, :],
                                              func=getattr(AF, SUB["silu_fn"])),
                                 r=(pgn[nn],), w=("sgb",))
                        if not SUB["up"]:
                            continue
                        for nn in range(2):
                            for ec in range(NE):
                                p.op("pe", lambda e, nn=nn, ec=ec, j=j:
                                     e.matmul(P[4 + nn][:, :],
                                              Wu[j % 2][:, ec * 128:(ec + 1) * 128],
                                              xn[ec][:, nn * 512:(nn + 1) * 512],
                                              start=(ec == 0), stop=(ec == 15)),
                                     r=(uslot, f"xn{ec}"),
                                     w=((f"P{4 + nn}",) if ec in (0, 15) else ()),
                                     inc=(MS["pe"] if ec == 15 else None))
                        if not SUB["hmul"]:
                            continue
                        for nn in range(2):
                            p.op("dve", lambda e, nn=nn:
                                 e.tensor_tensor(out=hst[:, nn * 512:(nn + 1) * 512],
                                                 in0=sgb[:, nn * 512:(nn + 1) * 512],
                                                 in1=P[4 + nn][:, :], op=ALU.mult),
                                 r=("sgb", f"P{4 + nn}"), w=("hst",))
                        if not SUB["sq"]:
                            continue
                        p.op("act", lambda e:
                             e.activation(out=h2b[:, :], in_=hst[:, :],
                                          func=AF.Square),
                             r=("hst",), w=("h2b",))
                        if not SUB["stats"]:
                            continue
                        for nn in range(2):
                            p.op("pe", lambda e, nn=nn, j=j:
                                 e.matmul(P[6 + nn][0:1, :], ones_bf[:, :],
                                          h2b[:, nn * 512:(nn + 1) * 512],
                                          start=(j == 0), stop=(j == SUB["nj"] - 1)),
                                 r=("h2b", "ones_bf"),
                                 w=((f"P{6 + nn}",) if j in (0, SUB["nj"] - 1) else ()),
                                 inc=MS["pe"])
                        if SUB["hdma"]:
                            p.dma("po",
                                  s_h[j * 128:(j + 1) * 128, ts * 1024:(ts + 1) * 1024],
                                  hst[:, :], r=("hst",), w=("s_h",), sem="d_hst")
                    for nn in range(2 if SUB["rms"] else 0):
                        p.op("act", lambda e, nn=nn:
                             e.activation(out=P[4 + nn][0:1, :], in_=P[6 + nn][0:1, :],
                                          func=AF.Sqrt, scale=1.0 / H, bias=cb[0:1, 1:2]),
                             r=(f"P{6 + nn}", "cb"), w=(f"P{4 + nn}",))
                        p.op("dve", lambda e, nn=nn:
                             e.reciprocal(out=r_S[0:1, nn * 512:(nn + 1) * 512],
                                          in_=P[4 + nn][0:1, :]),
                             r=(f"P{4 + nn}",), w=("r_S",))
                p.barrier()
                if not SUB["proj"]:
                    continue
                if True:
                    for half in range(2):
                        ts512 = ts * 2 + half
                        p.op("pe", lambda e, half=half:
                             e.matmul(P[0][:, :], ones05[:, :],
                                      r_S[0:1, half * 512:(half + 1) * 512],
                                      start=True, stop=True),
                             r=("r_S", "ones05"), w=("P0",))
                        p.op("act", lambda e:
                             e.activation(out=R_sb, in_=P[0][:, :],
                                          func=AF.Copy),
                             r=("P0",), w=("sgb",))
                        for dg in range(2):
                            for j2 in range(NJ):
                                wslot, hslot = wpk[j2 % 2], hrk[j2 % 2]
                                p.dma("sp", Wp[j2 % 2],
                                      s_wp[(dg * 64 + j2) * 128:
                                           (dg * 64 + j2 + 1) * 128, :], w=(wslot,))
                                p.dma("act", hrd[j2 % 2],
                                      s_h[j2 * 128:(j2 + 1) * 128,
                                          ts512 * 512:(ts512 + 1) * 512],
                                      w=(hslot,))
                                for d in range(8):
                                    p.op("pe", lambda e, j2=j2, d=d:
                                         e.matmul(P[d][:, :],
                                                  Wp[j2 % 2][:, d * 128:(d + 1) * 128],
                                                  hrd[j2 % 2],
                                                  start=(j2 == 0), stop=(j2 == NJ - 1)),
                                         r=(wslot, hslot),
                                         w=((f"P{d}",) if j2 in (0, NJ - 1) else ()),
                                         inc=(MS["pe"] if d == 7 or j2 == NJ - 1
                                              else None))
                            for d in range(8):
                                fo = dg * 8 + d
                                p.op("dve", lambda e, d=d:
                                     e.tensor_tensor(out=prj, in0=P[d][:, :],
                                                     in1=R_sb, op=ALU.mult),
                                     r=(f"P{d}", "sgb"), w=("hst",))
                                p.op("dve", lambda e, fo=fo, ts512=ts512:
                                     e.tensor_tensor(
                                         out=X[fo][:, ts512 * 512:(ts512 + 1) * 512],
                                         in0=X[fo][:, ts512 * 512:(ts512 + 1) * 512],
                                         in1=prj, op=ALU.add),
                                     r=("hst",), w=(f"X{fo}",))
                p.barrier()
        p.barrier()

    def attn():
        with ExitStack() as ax:
            asb = lambda nm, sh, dt: ax.enter_context(nc.sbuf_tensor(nm, sh, dt))
            rcols = asb("at_rcols", [128, 16], F32)
            with ExitStack() as bx:
                bsb = lambda nm, sh, dt: bx.enter_context(nc.sbuf_tensor(nm, sh, dt))
                r_mu = bsb("a_r_mu", [1, T], F32)
                r_A = bsb("a_r_A", [1, T], F32)
                Msb = bsb("a_Msb", [128, 512], F32)
                Rsb = bsb("a_Rsb", [128, 512], F32)
                xn = [bsb(f"a_xn{i}", [128, 1024], BF16) for i in range(NE)]
                ln_rows(r_mu, r_A, "att")
                Wsl = [bsb(f"a_Wsl{i}", [128, 4096], BF16) for i in range(2)]
                Wv = [bsb(f"a_Wv{i}", [128, 2048], BF16) for i in range(2)]
                qst = bsb("a_qst", [128, 512], BF16)
                vst = bsb("a_vst", [128, 512], BF16)
                for ts in range(2):
                    ln_apply(r_mu, r_A, xn, Msb, Rsb, ts)
                    for which, wsrc in ((0, s_wq), (1, s_wk)):
                        wr = wsrc.rearrange("(ec pp) f -> pp ec f", pp=128)
                        for fc in range(4):
                            for hh in range(2):
                                hsl = f"a_Wsl{hh}"
                                p.dma("sp", Wsl[hh][:, :].rearrange(
                                        "pp (ec f) -> pp ec f", ec=8),
                                      wr[:, hh * 8:(hh + 1) * 8,
                                         fc * 512:(fc + 1) * 512],
                                      w=(hsl,))
                            for sl in range(8):
                                pn = f"P{sl % 2}"
                                sc = ts * 8 + sl
                                for ec in range(NE):
                                    hh = ec // 8
                                    p.op("pe", lambda e, sl=sl, ec=ec, hh=hh:
                                         e.matmul(P[sl % 2][:, :],
                                                  xn[ec][:, sl * 128:(sl + 1) * 128],
                                                  Wsl[hh][:, (ec % 8) * 512:
                                                          (ec % 8 + 1) * 512],
                                                  start=(ec == 0), stop=(ec == 15)),
                                         r=(f"a_Wsl{hh}", f"xn{ec}"),
                                         w=((pn,) if ec in (0, 15) else ()),
                                         inc=(MS["pe"] if ec in (7, 15) else None))
                                if which == 0:
                                    p.op("act", lambda e, sl=sl:
                                         e.activation(out=qst[:, :],
                                                      in_=P[sl % 2][:, :],
                                                      func=AF.Copy, scale=0.125),
                                         r=(pn,), w=("a_qst",))
                                    dst = s_q[(sc * 16 + fc * 4) * 128:
                                              (sc * 16 + fc * 4 + 4) * 128,
                                              :].rearrange("(b pp) c -> pp b c",
                                                           pp=128)
                                    p.dma("po", dst,
                                          qst[:, :].rearrange("pp (b c) -> pp b c",
                                                              b=4),
                                          r=("a_qst",), w=("s_q",), sem="d_a_qst")
                                else:
                                    p.op("act", lambda e, sl=sl:
                                         e.activation(out=qst[:, :],
                                                      in_=P[sl % 2][:, :],
                                                      func=AF.Copy),
                                         r=(pn,), w=("a_qst",))
                                    p.dma("po", s_k[sc * 128:(sc + 1) * 128,
                                                    fc * 512:(fc + 1) * 512],
                                          qst[:, :], r=("a_qst",), w=("s_k",),
                                          sem="d_a_qst")
                    for fv in range(NE):
                        wslot = f"a_Wv{fv % 2}"
                        p.dma("sp", Wv[fv % 2][:, :],
                              s_wv2[fv * 128:(fv + 1) * 128, :], w=(wslot,))
                        for nn in range(2):
                            pn = f"P{2 + nn}"
                            for ec in range(NE):
                                p.op("pe", lambda e, fv=fv, nn=nn, ec=ec:
                                     e.matmul(P[2 + nn][:, :],
                                              Wv[fv % 2][:, ec * 128:(ec + 1) * 128],
                                              xn[ec][:, nn * 512:(nn + 1) * 512],
                                              start=(ec == 0), stop=(ec == 15)),
                                     r=(wslot, f"xn{ec}"),
                                     w=((pn,) if ec in (0, 15) else ()),
                                     inc=(MS["pe"] if ec == 15 else None))
                            p.op("act", lambda e, nn=nn:
                                 e.activation(out=vst[:, :], in_=P[2 + nn][:, :],
                                              func=AF.Copy),
                                 r=(pn,), w=("a_vst",))
                            p.dma("po", s_v[fv * 128:(fv + 1) * 128,
                                            ts * 1024 + nn * 512:
                                            ts * 1024 + (nn + 1) * 512],
                                  vst[:, :], r=("a_vst",), w=("s_v",), sem="d_a_vst")
            p.barrier()

            with ExitStack() as sx:
                ssb = lambda nm, sh, dt: sx.enter_context(nc.sbuf_tensor(nm, sh, dt))
                acc = ssb("st_acc", [128, 8], F32)
                srow = ssb("st_srow", [128, 1], F32)
                As = [ssb(f"st_As{i}", [128, T], F32) for i in range(2)]
                kb = [ssb(f"st_kb{i}", [128, T], BF16) for i in range(2)]
                qb = [ssb(f"st_qb{i}", [128, 256], BF16) for i in range(2)]
                ats = [ssb(f"st_ats{i}", [128, 128], BF16) for i in range(2)]
                for pr in range(8):
                    eoA, eoB = 2 * pr, 2 * pr + 1
                    for sc in range(16):
                        kslot, qslot = f"st_kb{sc % 2}", f"st_qb{sc % 2}"
                        p.dma("sp", kb[sc % 2][:, :],
                              s_k[sc * 128:(sc + 1) * 128, :], w=(kslot,))
                        qsrc = s_q[(sc * 16 + eoA) * 128:(sc * 16 + eoB + 1) * 128, :]
                        p.dma("sp",
                              qb[sc % 2][:, :].rearrange("pp (b c) -> pp b c", b=2),
                              qsrc.rearrange("(b pp) c -> pp b c", pp=128),
                              w=(qslot,))
                        for half in range(2):
                            for fc in range(4):
                                bank = half * 4 + fc
                                p.op("pe", lambda e, sc=sc, fc=fc, half=half,
                                     bank=bank:
                                     e.matmul(P[bank][:, :],
                                              qb[sc % 2][:, half * 128:
                                                         half * 128 + 128],
                                              kb[sc % 2][:, fc * 512:(fc + 1) * 512],
                                              start=(sc == 0), stop=(sc == 15)),
                                     r=(kslot, qslot),
                                     w=((f"P{bank}",) if sc in (0, 15) else ()),
                                     inc=(MS["pe"] if (sc == 15 or
                                                       (half, fc) == (1, 3))
                                          else None))
                    for half, eo in ((0, eoA), (1, eoB)):
                        for fc in range(4):
                            bank = half * 4 + fc
                            p.op("act", lambda e, half=half, fc=fc, bank=bank:
                                 e.activation(
                                     out=As[half][:, fc * 512:(fc + 1) * 512],
                                     in_=P[bank][:, :], func=AF.Exp, bias=cb[:, 2:3],
                                     accum_out=acc[:, bank:bank + 1]),
                                 r=(f"P{bank}",), w=(f"st_As{half}", "st_acc"))
                        p.op("dve", lambda e, half=half:
                             e.tensor_reduce(out=srow[:, 0:1],
                                             in_=acc[:, half * 4:half * 4 + 4],
                                             axis=AX.X, op=ALU.add),
                             r=("st_acc",), w=("st_srow",))
                        p.op("dve", lambda e, eo=eo:
                             e.reciprocal(out=rcols[:, eo:eo + 1], in_=srow[:, 0:1]),
                             r=("st_srow",), w=("at_rcols",))
                        for b in range(16):
                            bank = half * 4 + b // 4
                            p.op("pe", lambda e, half=half, b=b, bank=bank:
                                 e.transpose(
                                     P[bank][:, (b % 4) * 128:(b % 4 + 1) * 128],
                                     As[half][:, b * 128:(b + 1) * 128],
                                     eye_sb[:, :]),
                                 r=(f"st_As{half}", "eye_sb"),
                                 w=((f"P{bank}",) if b % 4 == 0 else ()),
                                 inc=(MS["pe"] if b % 4 == 3 else None))
                        for b in range(16):
                            bank = half * 4 + b // 4
                            aslot = f"st_ats{b % 2}"
                            p.op("act", lambda e, b=b, bank=bank:
                                 e.activation(
                                     out=ats[b % 2][:, :],
                                     in_=P[bank][:, (b % 4) * 128:(b % 4 + 1) * 128],
                                     func=AF.Copy),
                                 r=(f"P{bank}",), w=(aslot,))
                            p.dma("po",
                                  s_at[(b * 16 + eo) * 128:(b * 16 + eo + 1) * 128, :],
                                  ats[b % 2][:, :], r=(aslot,), w=("s_at",),
                                  sem="d_" + aslot)
            p.barrier()

            with ExitStack() as wx:
                wsb = lambda nm, sh, dt: wx.enter_context(nc.sbuf_tensor(nm, sh, dt))
                attsb = [wsb(f"w_att{i}", [128, 512], BF16) for i in range(16)]
                ATb = [wsb(f"w_AT{i}", [128, 1024], BF16) for i in range(2)]
                vt = [wsb(f"w_vt{i}", [128, 512], BF16) for i in range(2)]
                Wob = [wsb(f"w_Wo{i}", [128, 2048], BF16) for i in range(2)]
                for ss in range(4):
                    for eg in range(2):
                        for fc in range(16):
                            aslot, vslot = f"w_AT{fc % 2}", f"w_vt{fc % 2}"
                            strip = s_at[(fc * 16 + eg * 8) * 128:
                                         (fc * 16 + eg * 8 + 8) * 128, :]
                            p.dma("sp",
                                  ATb[fc % 2][:, :].rearrange(
                                      "pp (b c) -> pp b c", b=8),
                                  strip.rearrange("(b pp) c -> pp b c", pp=128),
                                  w=(aslot,))
                            p.dma("act", vt[fc % 2][:, :],
                                  s_v[fc * 128:(fc + 1) * 128,
                                      ss * 512:(ss + 1) * 512], w=(vslot,))
                            for d in range(8):
                                p.op("pe", lambda e, fc=fc, d=d:
                                     e.matmul(P[d][:, :],
                                              ATb[fc % 2][:, d * 128:(d + 1) * 128],
                                              vt[fc % 2][:, :],
                                              start=(fc == 0), stop=(fc == 15)),
                                     r=(aslot, vslot),
                                     w=((f"P{d}",) if fc in (0, 15) else ()),
                                     inc=(MS["pe"] if d == 7 or fc == 15 else None))
                        for d in range(8):
                            eo = eg * 8 + d
                            p.op("dve", lambda e, d=d, eo=eo:
                                 e.tensor_scalar(out=attsb[eo][:, :], in0=P[d][:, :],
                                                 scalar1=rcols[:, eo:eo + 1],
                                                 scalar2=None, op0=ALU.mult),
                                 r=(f"P{d}", "at_rcols"), w=(f"w_att{eo}",))
                    for fo in range(16):
                        wslot = f"w_Wo{fo % 2}"
                        p.dma("sp", Wob[fo % 2][:, :],
                              s_wo2[fo * 128:(fo + 1) * 128, :], w=(wslot,))
                        pn = f"P{fo % 2}"
                        for ec in range(NE):
                            p.op("pe", lambda e, fo=fo, ec=ec:
                                 e.matmul(P[fo % 2][:, :],
                                          Wob[fo % 2][:, ec * 128:(ec + 1) * 128],
                                          attsb[ec][:, :],
                                          start=(ec == 0), stop=(ec == 15)),
                                 r=(wslot, f"w_att{ec}"),
                                 w=((pn,) if ec in (0, 15) else ()),
                                 inc=(MS["pe"] if ec == 15 else None))
                        p.op("dve", lambda e, fo=fo, ss=ss:
                             e.scalar_tensor_tensor(
                                 out=X[fo][:, ss * 512:(ss + 1) * 512],
                                 in0=P[fo % 2][:, :], scalar=1.0,
                                 in1=X[fo][:, ss * 512:(ss + 1) * 512],
                                 op0=ALU.mult, op1=ALU.add),
                             r=(pn,), w=(f"X{fo}",))
            p.barrier()

    if "ffn1" in phases:
        ffn_pass("f1")
    elif "quant" in phases:
        pass
    if want_attn:
        attn()
    if "ffn2" in phases:
        ffn_pass("f2")

    for i in range(NE):
        p.dma("po", yT[i * 128:(i + 1) * 128, :], X[i][:, :], r=(f"X{i}",),
              sem="d_out")
    p.emit("po", lambda e: e.nop(), [("d_out", p.cnt["d_out"])], None, 1)

    sems = {s: es.enter_context(nc.semaphore(s)) for s in list(SEMS) + p.dsems}

    def replay(name):
        def run(eng):
            for fn, waits, inc, n in p.streams[name]:
                for s, v in waits:
                    eng.wait_ge(sems[s], v)
                inst = fn(eng)
                if inc is not None:
                    inst.then_inc(sems[inc], n)
        return run

    with nc.Block() as block:
        block.sync(replay("sp"))
        block.tensor(replay("pe"))
        block.scalar(replay("act"))
        block.vector(replay("dve"))
        block.gpsimd(replay("po"))

    es.close()
    build.last_prog = p
    return nc


_BUILD_CACHE = {}


def _get_nc(phases):
    key = tuple(phases)
    if key not in _BUILD_CACHE:
        _BUILD_CACHE[key] = build(phases)
    return _BUILD_CACHE[key]


def kernel(L0_attn_x, L0_attn_Wq, L0_attn_Wk, L0_attn_Wv, L0_attn_Wo,
           L0_ffn_w_gate_up, L0_ffn_w_proj):
    x = np.asarray(L0_attn_x, dtype=np.float32)
    wq = np.ascontiguousarray(np.asarray(L0_attn_Wq, dtype=np.float32))
    wk = np.ascontiguousarray(np.asarray(L0_attn_Wk, dtype=np.float32))
    wv = np.ascontiguousarray(np.asarray(L0_attn_Wv, dtype=np.float32))
    wo = np.ascontiguousarray(np.asarray(L0_attn_Wo, dtype=np.float32))
    wguT = np.ascontiguousarray(np.asarray(L0_ffn_w_gate_up, dtype=np.float32).T)
    wpT = np.ascontiguousarray(np.asarray(L0_ffn_w_proj, dtype=np.float32).T)
    ind16, sel16, eye = _consts_np()

    nc = _get_nc(PHASES)
    in_maps = []
    for i in range(8):
        in_maps.append({
            "xT": np.ascontiguousarray(x[i].T),
            "wq": wq, "wk": wk, "wv": wv, "wo": wo,
            "wguT": wguT, "wpT": wpT,
            "eye_f": eye, "ind16": ind16, "sel16": sel16,
        })
    res = run_bass_kernel_spmd(nc, in_maps, core_ids=list(range(8)))
    out = np.empty((8, T, D), dtype=np.float32)
    for i in range(8):
        out[i] = res.results[i]["yT"].T
    return out


# version 02
# speedup vs baseline: 1.2399x; 1.2399x over previous
import numpy as np
import ml_dtypes
from contextlib import ExitStack

import concourse.bass as bass
import concourse.mybir as mybir
from concourse.bass_utils import run_bass_kernel_spmd

F32 = mybir.dt.float32
BF16 = mybir.dt.bfloat16
AF = mybir.ActivationFunctionType
ALU = mybir.AluOpType
AX = mybir.AxisListType
BF = ml_dtypes.bfloat16

D = 2048
T = 2048
NE = 16
H = 8192
NJ = 64
THRC = float(0.5 + 2.0 ** -10)
LN_EPS = 1e-5
RMS_EPS = float(np.finfo(np.float32).eps)
EXP_BIAS = -15.0

PHASES = ("ffn1", "attn", "ffn2")
SUB = {"ln": True, "apply": True, "gu": True, "proj": True, "nj": NJ, "silu": True, "up": True, "hmul": True, "sq": True, "stats": True, "hdma": True, "rms": True, "silu_fn": "Silu", "noquant": False}

ENGS = ("sp", "pe", "act", "dve", "po")
MS = {"sp": "msp", "pe": "mpe", "act": "mact", "dve": "mdve", "po": "mpo"}
SEMS = ("msp", "mpe", "mact", "mdve", "mpo")


class Buf:
    __slots__ = ("w", "r")

    def __init__(self):
        self.w = None
        self.r = {}


class Prog:
    def __init__(self):
        self.streams = {e: [] for e in ENGS}
        self.cnt = {s: 0 for s in SEMS}
        self.dsems = []
        self.ewait = {e: {} for e in ENGS}
        self.bufs = {}
        self.pend = {e: [] for e in ENGS}

    def dsem(self, name):
        if name not in self.cnt:
            self.cnt[name] = 0
            self.dsems.append(name)
        return name

    def fence_now(self):
        f = dict(getattr(self, "fence", ()))
        for b in self.bufs.values():
            if b.w:
                s, v = b.w
                f[s] = max(f.get(s, 0), v)
            for s, v in b.r.items():
                f[s] = max(f.get(s, 0), v)
        self.fence = tuple(f.items())

    def buf(self, name):
        b = self.bufs.get(name)
        if b is None:
            b = self.bufs[name] = Buf()
        return b

    def emit(self, eng, fn, waits, inc, n):
        wl = {}
        for s, v in waits:
            if v > 0:
                wl[s] = max(wl.get(s, 0), v)
        eh = self.ewait[eng]
        final = []
        for s, v in wl.items():
            if v > eh.get(s, 0):
                assert v <= self.cnt[s], \
                    f"{eng}: wait {s}>={v} unreachable (emitted {self.cnt[s]})"
                final.append((s, v))
                eh[s] = v
        after = None
        if inc is not None:
            self.cnt[inc] += n
            after = self.cnt[inc]
        self.streams[eng].append((fn, tuple(final), inc, n))
        return after

    def op(self, eng, fn, r=(), w=(), inc="default", n=1):
        waits = []
        for name in r:
            b = self.buf(name)
            if b.w:
                waits.append(b.w)
        for name in w:
            b = self.buf(name)
            if b.w:
                waits.append(b.w)
            waits.extend(b.r.items())
            if b.w is None and not b.r:
                waits.extend(getattr(self, "fence", ()))
        if inc == "default":
            inc = MS[eng]
        after = self.emit(eng, fn, waits, inc, n)
        if inc is None:
            self.pend[eng].append((tuple(r), tuple(w)))
        else:
            for (pr, pw) in self.pend[eng] + [(tuple(r), tuple(w))]:
                for name in pr:
                    b = self.buf(name)
                    b.r[inc] = max(b.r.get(inc, 0), after)
                for name in pw:
                    b = self.buf(name)
                    b.w = (inc, after)
                    b.r = {}
            self.pend[eng] = []
        return after

    def dma(self, q, dst, src, r=(), w=(), sem=None):
        if sem is None:
            assert w, "dma needs w= slot or explicit sem"
            sem = "d_" + w[0]
        return self.op(q, lambda e, a=dst, b=src: e.dma_start(out=a, in_=b),
                       r=r, w=w, inc=self.dsem(sem), n=16)

    def barrier(self):
        for eng in ENGS:
            assert not self.pend[eng], f"pending notes on {eng} at barrier"
        waits = [(MS[o], self.cnt[MS[o]]) for o in ENGS]
        waits += [(s, self.cnt[s]) for s in self.dsems]
        for eng in ENGS:
            self.emit(eng, lambda e: e.nop(),
                      [wv for wv in waits if wv[0] != MS[eng]], None, 1)
        for b in self.bufs.values():
            b.w = None
            b.r = {}
        self.fence = ()


def _consts_np():
    ind16 = np.zeros((128, 512), dtype=BF)
    sel16 = np.zeros((32, 2048), dtype=BF)
    for ec in range(16):
        ind16[0:64, ec * 32 + 2 * ec] = 1
        ind16[64:128, ec * 32 + 2 * ec + 1] = 1
        sel16[2 * ec, ec * 128: ec * 128 + 64] = 1
        sel16[2 * ec + 1, ec * 128 + 64: ec * 128 + 128] = 1
    eye = np.eye(128, dtype=np.float32)
    return ind16, sel16, eye


def build(phases=None):
    if phases is None:
        phases = PHASES
    want_ffn = ("ffn1" in phases) or ("ffn2" in phases) or ("quant" in phases)
    want_attn = "attn" in phases

    nc = bass.Bass("TRN2", target_bir_lowering=False)
    p = Prog()

    xT = nc.dram_tensor("xT", [D, T], F32, kind="ExternalInput").ap()
    yT = nc.dram_tensor("yT", [D, T], F32, kind="ExternalOutput").ap()
    wq_d = nc.dram_tensor("wq", [D, D], F32, kind="ExternalInput").ap()
    wk_d = nc.dram_tensor("wk", [D, D], F32, kind="ExternalInput").ap()
    wv_d = nc.dram_tensor("wv", [D, D], F32, kind="ExternalInput").ap()
    wo_d = nc.dram_tensor("wo", [D, D], F32, kind="ExternalInput").ap()
    wguT = nc.dram_tensor("wguT", [D, 2 * H], F32, kind="ExternalInput").ap()
    wpT = nc.dram_tensor("wpT", [H, D], F32, kind="ExternalInput").ap()
    eye_d = nc.dram_tensor("eye_f", [128, 128], F32, kind="ExternalInput").ap()
    ind16_d = nc.dram_tensor("ind16", [128, 512], BF16, kind="ExternalInput").ap()
    sel16_d = nc.dram_tensor("sel16", [32, 2048], BF16, kind="ExternalInput").ap()

    s_wgu = nc.dram_tensor("s_wgu", [128 * 128, 2048], BF16, kind="Internal").ap()
    s_wp = nc.dram_tensor("s_wp", [128 * 128, 1024], BF16, kind="Internal").ap()
    s_h = nc.dram_tensor("s_h", [H, T], BF16, kind="Internal").ap()
    s_wq = nc.dram_tensor("s_wq", [D, D], BF16, kind="Internal").ap()
    s_wk = nc.dram_tensor("s_wk", [D, D], BF16, kind="Internal").ap()
    s_wv2 = nc.dram_tensor("s_wv2", [D, D], BF16, kind="Internal").ap()
    s_wo2 = nc.dram_tensor("s_wo2", [D, D], BF16, kind="Internal").ap()
    s_q = nc.dram_tensor("s_q", [16 * 16 * 128, 128], BF16, kind="Internal").ap()
    s_k = nc.dram_tensor("s_k", [T, D], BF16, kind="Internal").ap()
    s_v = nc.dram_tensor("s_v", [D, T], BF16, kind="Internal").ap()
    s_at = nc.dram_tensor("s_at", [16 * 16 * 128, 128], BF16, kind="Internal").ap()

    es = ExitStack()

    def sb(name, shape, dt):
        return es.enter_context(nc.sbuf_tensor(name, shape, dt))

    def ps(name, shape, dt):
        return es.enter_context(nc.psum_tensor(name, shape, dt))

    X = [sb(f"X{i}", [128, T], F32) for i in range(NE)]
    onesK1 = sb("onesK1", [1, 128], F32)
    ones05 = sb("ones05", [1, 128], F32)
    ones_bf = sb("ones_bf", [128, 1], BF16)
    eye_sb = sb("eye_sb", [128, 128], F32)
    cb = sb("cb", [128, 3], F32)

    P = [ps(f"P{i}", [128, 512], F32) for i in range(8)]

    p.op("dve", lambda e: e.memset(onesK1[:, :], 1.0), w=("onesK1",))
    p.op("dve", lambda e: e.memset(ones05[:, :], 0.5), w=("ones05",))
    p.op("dve", lambda e: e.memset(ones_bf[:, :], 1.0), w=("ones_bf",))
    p.op("dve", lambda e: e.memset(cb[:, 0:1], LN_EPS), w=("cb",))
    p.op("dve", lambda e: e.memset(cb[:, 1:2], RMS_EPS), w=("cb",))
    p.op("dve", lambda e: e.memset(cb[:, 2:3], EXP_BIAS), w=("cb",))
    p.dma("sp", eye_sb[:, :], eye_d, w=("eye_sb",))

    for i in range(NE):
        p.dma("sp", X[i][:, :], xT[i * 128:(i + 1) * 128, :], w=(f"X{i}",),
              sem="d_xload")
    for i in range(NE):
        p.bufs[f"X{i}"].w = ("d_xload", p.cnt["d_xload"])

    if want_attn:
        p.dma("po", s_wq, wq_d, sem="d_cast")
        p.dma("po", s_wk, wk_d, sem="d_cast")
        wv_r = wv_d.rearrange("(ec pp) (fo c) -> fo pp ec c", pp=128, c=128)
        wo_r = wo_d.rearrange("(ec pp) (fo c) -> fo pp ec c", pp=128, c=128)
        sv_r = s_wv2.rearrange("(fo pp) (ec c) -> fo pp ec c", pp=128, c=128)
        so_r = s_wo2.rearrange("(fo pp) (ec c) -> fo pp ec c", pp=128, c=128)
        for fo in range(16):
            p.dma("po", sv_r[fo], wv_r[fo], sem="d_cast")
            p.dma("po", so_r[fo], wo_r[fo], sem="d_cast")
        for nmx in ("s_wq", "s_wk", "s_wv2", "s_wo2"):
            p.buf(nmx).w = ("d_cast", p.cnt["d_cast"])

    if want_ffn and not SUB["noquant"]:
        with ExitStack() as qx:
            qsb = lambda nm, sh, dt: qx.enter_context(nc.sbuf_tensor(nm, sh, dt))
            ind16 = qsb("ind16s", [128, 512], BF16)
            sel16 = qsb("sel16s", [32, 2048], BF16)
            wt = [qsb(f"q_wt{i}", [128, 1024], BF16) for i in range(16)]
            aw = [qsb(f"q_aw{i}", [128, 1024], BF16) for i in range(2)]
            s_bf = qsb("q_sbf", [32, 1024], BF16)
            S_sb = [qsb(f"q_Ssb{i}", [128, 1024], BF16) for i in range(2)]
            m1 = [qsb(f"q_m1{i}", [128, 1024], BF16) for i in range(2)]
            m2 = [qsb(f"q_m2{i}", [128, 1024], BF16) for i in range(2)]
            Tp = [qsb(f"q_T{i}", [128, 1024], BF16) for i in range(2)]
            wh = [qsb(f"q_wh{i}", [128, 1024], BF16) for i in range(2)]

            p.dma("sp", ind16[:, :], ind16_d, w=("ind16",))
            p.dma("sp", sel16[:, :], sel16_d, w=("sel16",))

            jobs = [("gu", st) for st in range(16)] + [("wp", st) for st in range(8)]
            for kind, st in jobs:
                for ec in range(16):
                    wslot = f"q_wt{ec}"
                    if kind == "gu":
                        src = wguT[ec * 128:(ec + 1) * 128, st * 1024:(st + 1) * 1024]
                    else:
                        hg, dgq = st // 2, st % 2
                        hq = hg * 16 + ec
                        src = wpT[hq * 128:(hq + 1) * 128, dgq * 1024:(dgq + 1) * 1024]
                    p.dma("po", wt[ec][:, :], src, w=(wslot,))
                    p.op("act", lambda e, ec=ec:
                         e.activation(out=aw[ec % 2][:, :], in_=wt[ec][:, :],
                                      func=AF.Abs),
                         r=(wslot,), w=(f"q_aw{ec % 2}",))
                    for nn in range(2):
                        p.op("pe", lambda e, ec=ec, nn=nn:
                             e.matmul(P[nn][0:32, :], ind16[:, ec * 32:(ec + 1) * 32],
                                      aw[ec % 2][:, nn * 512:(nn + 1) * 512],
                                      start=(ec == 0), stop=(ec == 15)),
                             r=(f"q_aw{ec % 2}", "ind16"),
                             w=((f"P{nn}",) if ec in (0, 15) else ()),
                             inc=(MS["pe"] if nn == 1 else None))
                for nn in range(2):
                    p.op("dve", lambda e, nn=nn:
                         e.tensor_scalar(out=s_bf[:, nn * 512:(nn + 1) * 512],
                                         in0=P[nn][0:32, :], scalar1=1.0 / 64,
                                         scalar2=1e-8, op0=ALU.mult, op1=ALU.max),
                         r=(f"P{nn}",), w=("q_sbf",))
                for ec in range(16):
                    sslot, mslot, nslot, whs = (f"q_Ssb{ec % 2}", f"q_m1{ec % 2}",
                                                f"q_m2{ec % 2}", f"q_wh{ec % 2}")
                    p.op("pe", lambda e, ec=ec:
                         e.matmul(P[2 + ec % 2][:, :],
                                  sel16[:, ec * 128:(ec + 1) * 128],
                                  s_bf[:, 0:512], start=True, stop=True),
                         r=("q_sbf", "sel16"), w=(f"P{2 + ec % 2}",))
                    p.op("act", lambda e, ec=ec:
                         e.activation(out=S_sb[ec % 2][:, 0:512],
                                      in_=P[2 + ec % 2][:, :], func=AF.Copy),
                         r=(f"P{2 + ec % 2}",), w=(sslot,))
                    p.op("pe", lambda e, ec=ec:
                         e.matmul(P[4 + ec % 2][:, :],
                                  sel16[:, ec * 128:(ec + 1) * 128],
                                  s_bf[:, 512:1024], start=True, stop=True),
                         r=("q_sbf", "sel16"), w=(f"P{4 + ec % 2}",))
                    p.op("act", lambda e, ec=ec:
                         e.activation(out=S_sb[ec % 2][:, 512:1024],
                                      in_=P[4 + ec % 2][:, :], func=AF.Copy),
                         r=(f"P{4 + ec % 2}",), w=(sslot,))
                    tslot = f"q_T{ec % 2}"
                    p.op("dve", lambda e, ec=ec:
                         e.tensor_scalar(out=Tp[ec % 2][:, :], in0=S_sb[ec % 2][:, :],
                                         scalar1=THRC, scalar2=None, op0=ALU.mult),
                         r=(sslot,), w=(tslot,))
                    p.op("dve", lambda e, ec=ec:
                         e.tensor_tensor(out=m1[ec % 2][:, :], in0=Tp[ec % 2][:, :],
                                         in1=wt[ec][:, :], op=ALU.is_lt),
                         r=(tslot, f"q_wt{ec}"), w=(mslot,))
                    p.op("dve", lambda e, ec=ec:
                         e.tensor_scalar(out=Tp[ec % 2][:, :], in0=S_sb[ec % 2][:, :],
                                         scalar1=-THRC, scalar2=None, op0=ALU.mult),
                         r=(sslot,), w=(tslot,))
                    p.op("dve", lambda e, ec=ec:
                         e.tensor_tensor(out=m2[ec % 2][:, :], in0=Tp[ec % 2][:, :],
                                         in1=wt[ec][:, :], op=ALU.is_gt),
                         r=(tslot, f"q_wt{ec}"), w=(nslot,))
                    p.op("dve", lambda e, ec=ec:
                         e.tensor_tensor(out=m1[ec % 2][:, :], in0=m1[ec % 2][:, :],
                                         in1=m2[ec % 2][:, :], op=ALU.subtract),
                         r=(nslot,), w=(mslot,))
                    p.op("dve", lambda e, ec=ec:
                         e.tensor_tensor(out=wh[ec % 2][:, :], in0=m1[ec % 2][:, :],
                                         in1=S_sb[ec % 2][:, :], op=ALU.mult),
                         r=(mslot, sslot), w=(whs,))
                    if kind == "gu":
                        j0 = st * 8
                        dst = s_wgu[j0 * 128:(j0 + 8) * 128,
                                    ec * 128:(ec + 1) * 128].rearrange(
                                        "(j pp) c -> pp j c", pp=128)
                        srcap = wh[ec % 2][:, :].rearrange("pp (j c) -> pp j c", j=8)
                        p.dma("po", dst, srcap, r=(whs,), w=("s_wgu",),
                              sem="d_" + whs)
                    else:
                        hg, dgq = st // 2, st % 2
                        hq = hg * 16 + ec
                        row0 = (dgq * 64 + hq) * 128
                        p.dma("po", s_wp[row0:row0 + 128, :], wh[ec % 2][:, :],
                              r=(whs,), w=("s_wp",), sem="d_" + whs)
        p.barrier()

    def ln_rows(r_mu, r_A, tag):
        with ExitStack() as lx:
            lsb = lambda nm, sh, dt: lx.enter_context(nc.sbuf_tensor(nm, sh, dt))
            xbf = [lsb(f"xbf{i}_{tag}", [128, T], BF16) for i in range(2)]
            sq = [lsb(f"sq{i}_{tag}", [128, T], BF16) for i in range(2)]
            for ec in range(NE):
                xs, qs = f"lnx{ec % 2}", f"lnq{ec % 2}"
                p.op("act", lambda e, ec=ec: e.activation(
                    out=xbf[ec % 2][:, :], in_=X[ec][:, :], func=AF.Copy),
                    r=(f"X{ec}",), w=(xs,))
                p.op("act", lambda e, ec=ec: e.activation(
                    out=sq[ec % 2][:, :], in_=xbf[ec % 2][:, :], func=AF.Square),
                    r=(xs,), w=(qs,))
                for nn in range(4):
                    p.op("pe", lambda e, ec=ec, nn=nn:
                         e.matmul(P[4 + nn][0:1, :], ones_bf[:, :],
                                  xbf[ec % 2][:, nn * 512:(nn + 1) * 512],
                                  start=(ec == 0), stop=(ec == 15)),
                         r=(xs, "ones_bf"),
                         w=((f"P{4 + nn}",) if ec in (0, 15) else ()),
                         inc=None)
                for nn in range(4):
                    p.op("pe", lambda e, ec=ec, nn=nn:
                         e.matmul(P[nn][0:1, :], ones_bf[:, :],
                                  sq[ec % 2][:, nn * 512:(nn + 1) * 512],
                                  start=(ec == 0), stop=(ec == 15)),
                         r=(qs, "ones_bf"),
                         w=((f"P{nn}",) if ec in (0, 15) else ()),
                         inc=(MS["pe"] if nn == 3 else None))
        p.fence_now()
        for nn in range(4):
            p.op("act", lambda e, nn=nn:
                 e.activation(out=r_mu[0:1, nn * 512:(nn + 1) * 512],
                              in_=P[4 + nn][0:1, :], func=AF.Copy, scale=1.0 / D),
                 r=(f"P{4 + nn}",), w=("r_mu",))
        p.op("dve", lambda e:
             e.tensor_tensor(out=r_A[0:1, :], in0=r_mu[0:1, :], in1=r_mu[0:1, :],
                             op=ALU.mult),
             r=("r_mu",), w=("r_A",))
        for nn in range(4):
            p.op("dve", lambda e, nn=nn:
                 e.scalar_tensor_tensor(out=r_A[0:1, nn * 512:(nn + 1) * 512],
                                        in0=P[nn][0:1, :], scalar=1.0 / D,
                                        in1=r_A[0:1, nn * 512:(nn + 1) * 512],
                                        op0=ALU.mult, op1=ALU.subtract),
                 r=(f"P{nn}",), w=("r_A",))
        for nn in range(4):
            p.op("act", lambda e, nn=nn:
                 e.activation(out=P[nn][0:1, :],
                              in_=r_A[0:1, nn * 512:(nn + 1) * 512],
                              func=AF.Sqrt, bias=cb[0:1, 0:1]),
                 r=("r_A", "cb"), w=(f"P{nn}",))
        for nn in range(4):
            p.op("dve", lambda e, nn=nn:
                 e.reciprocal(out=r_A[0:1, nn * 512:(nn + 1) * 512],
                              in_=P[nn][0:1, :]),
                 r=(f"P{nn}",), w=("r_A",))

    def ln_apply(r_mu, r_A, xn, Msb, Rsb, ts):
        for hf in range(2):
            base = ts * 1024 + hf * 512
            p.op("pe", lambda e, base=base:
                 e.matmul(P[6][:, :], onesK1[:, :], r_mu[0:1, base:base + 512],
                          start=True, stop=True),
                 r=("r_mu", "onesK1"), w=("P6",))
            p.op("pe", lambda e, base=base:
                 e.matmul(P[7][:, :], onesK1[:, :], r_A[0:1, base:base + 512],
                          start=True, stop=True),
                 r=("r_A", "onesK1"), w=("P7",))
            p.op("act", lambda e:
                 e.activation(out=Msb[:, :], in_=P[6][:, :], func=AF.Copy),
                 r=("P6",), w=("Msb",))
            p.op("act", lambda e:
                 e.activation(out=Rsb[:, :], in_=P[7][:, :], func=AF.Copy),
                 r=("P7",), w=("Rsb",))
            for ec in range(NE):
                sl = slice(hf * 512, hf * 512 + 512)
                p.op("dve", lambda e, ec=ec, sl=sl, base=base:
                     e.scalar_tensor_tensor(out=xn[ec][:, sl],
                                            in0=X[ec][:, base:base + 512], scalar=1.0,
                                            in1=Msb[:, :], op0=ALU.mult,
                                            op1=ALU.subtract),
                     r=(f"X{ec}", "Msb"), w=(f"xn{ec}",))
                p.op("dve", lambda e, ec=ec, sl=sl:
                     e.tensor_tensor(out=xn[ec][:, sl], in0=xn[ec][:, sl],
                                     in1=Rsb[:, :], op=ALU.mult),
                     r=("Rsb",), w=(f"xn{ec}",))

    def ffn_pass(tag):
        with ExitStack() as fx:
            fsb = lambda nm, sh, dt: fx.enter_context(
                nc.sbuf_tensor(nm + "_" + tag, sh, dt))
            r_mu = fsb("r_mu", [1, T], F32)
            r_A = fsb("r_A", [1, T], F32)
            r_S = fsb("r_S", [1, 1024], F32)
            Msb = fsb("Msb", [128, 512], F32)
            Rsb = fsb("Rsb", [128, 512], F32)
            xn = [fsb(f"xn{i}", [128, 1024], BF16) for i in range(NE)]

            if SUB["ln"]:
                ln_rows(r_mu, r_A, tag)
            Wg = [fsb(f"Wg{i}", [128, 2048], BF16) for i in range(2)]
            Wu = [fsb(f"Wu{i}", [128, 2048], BF16) for i in range(2)]
            sgb = fsb("sgb", [128, 1024], BF16)
            hst = fsb("hst", [128, 1024], BF16)
            h2b = [fsb("h2b0", [128, 1024], BF16), Msb[:, :].bitcast(BF16)]
            h2k = ("h2b0", "Msb")
            Wp = [Wg[0][:, 0:1024], Wg[0][:, 1024:2048],
                  Wg[1][:, 0:1024], Wg[1][:, 1024:2048]]
            wpk = ("Wg0a", "Wg0b", "Wg1a", "Wg1b")
            hrd = [Wu[0][:, 0:512], Wu[0][:, 512:1024],
                   Wu[0][:, 1024:1536], Wu[0][:, 1536:2048]]
            hrk = ("Wu0a", "Wu0b", "Wu0c", "Wu0d")
            R_sb = sgb[:, :].bitcast(F32)
            prj = hst[:, :].bitcast(F32)
            for ts in range(2):
                if not SUB["apply"]:
                    continue
                if SUB["ln"]:
                    ln_apply(r_mu, r_A, xn, Msb, Rsb, ts)
                if not SUB["gu"]:
                    continue
                if True:
                    for j in range(SUB["nj"]):
                        gslot, uslot = f"Wg{j % 2}", f"Wu{j % 2}"
                        pgn = ("P0", "P1") if j % 2 == 0 else ("P2", "P3")
                        pg = (P[0], P[1]) if j % 2 == 0 else (P[2], P[3])
                        p.dma("sp", Wg[j % 2][:, :],
                              s_wgu[j * 128:(j + 1) * 128, :], w=(gslot,))
                        p.dma("sp", Wu[j % 2][:, :],
                              s_wgu[(64 + j) * 128:(64 + j + 1) * 128, :], w=(uslot,))
                        for nn in range(2):
                            for ec in range(NE):
                                p.op("pe", lambda e, nn=nn, ec=ec, pg=pg, j=j:
                                     e.matmul(pg[nn][:, :],
                                              Wg[j % 2][:, ec * 128:(ec + 1) * 128],
                                              xn[ec][:, nn * 512:(nn + 1) * 512],
                                              start=(ec == 0), stop=(ec == 15)),
                                     r=(gslot, f"xn{ec}"),
                                     w=((pgn[nn],) if ec in (0, 15) else ()),
                                     inc=(MS["pe"] if ec == 15 else None))
                        if SUB["silu"]:
                          for nn in range(2):
                            p.op("act", lambda e, nn=nn, pg=pg:
                                 e.activation(out=sgb[:, nn * 512:(nn + 1) * 512],
                                              in_=pg[nn][:, :],
                                              func=getattr(AF, SUB["silu_fn"])),
                                 r=(pgn[nn],), w=("sgb",))
                        if not SUB["up"]:
                            continue
                        for nn in range(2):
                            for ec in range(NE):
                                p.op("pe", lambda e, nn=nn, ec=ec, j=j:
                                     e.matmul(P[4 + nn][:, :],
                                              Wu[j % 2][:, ec * 128:(ec + 1) * 128],
                                              xn[ec][:, nn * 512:(nn + 1) * 512],
                                              start=(ec == 0), stop=(ec == 15)),
                                     r=(uslot, f"xn{ec}"),
                                     w=((f"P{4 + nn}",) if ec in (0, 15) else ()),
                                     inc=(MS["pe"] if ec == 15 else None))
                        if not SUB["hmul"]:
                            continue
                        for nn in range(2):
                            p.op("dve", lambda e, nn=nn:
                                 e.tensor_tensor(out=hst[:, nn * 512:(nn + 1) * 512],
                                                 in0=sgb[:, nn * 512:(nn + 1) * 512],
                                                 in1=P[4 + nn][:, :], op=ALU.mult),
                                 r=("sgb", f"P{4 + nn}"), w=("hst",))
                        if not SUB["sq"]:
                            continue
                        p.op("act", lambda e, j=j:
                             e.activation(out=h2b[j % 2][:, :], in_=hst[:, :],
                                          func=AF.Square),
                             r=("hst",), w=(h2k[j % 2],))
                        if not SUB["stats"]:
                            continue
                        def emit_stats(jj):
                            for nn in range(2):
                                p.op("pe", lambda e, nn=nn, jj=jj:
                                     e.matmul(P[6 + nn][0:1, :], ones_bf[:, :],
                                              h2b[jj % 2][:, nn * 512:(nn + 1) * 512],
                                              start=(jj == 0),
                                              stop=(jj == SUB["nj"] - 1)),
                                     r=(h2k[jj % 2], "ones_bf"),
                                     w=((f"P{6 + nn}",)
                                        if jj in (0, SUB["nj"] - 1) else ()),
                                     inc=MS["pe"])
                        if j > 0:
                            emit_stats(j - 1)
                        if j == SUB["nj"] - 1:
                            emit_stats(j)
                        if SUB["hdma"]:
                            p.dma("po",
                                  s_h[j * 128:(j + 1) * 128, ts * 1024:(ts + 1) * 1024],
                                  hst[:, :], r=("hst",), w=("s_h",), sem="d_hst")
                    for nn in range(2 if SUB["rms"] else 0):
                        p.op("act", lambda e, nn=nn:
                             e.activation(out=P[4 + nn][0:1, :], in_=P[6 + nn][0:1, :],
                                          func=AF.Sqrt, scale=1.0 / H, bias=cb[0:1, 1:2]),
                             r=(f"P{6 + nn}", "cb"), w=(f"P{4 + nn}",))
                        p.op("dve", lambda e, nn=nn:
                             e.reciprocal(out=r_S[0:1, nn * 512:(nn + 1) * 512],
                                          in_=P[4 + nn][0:1, :]),
                             r=(f"P{4 + nn}",), w=("r_S",))
                p.barrier()
                if not SUB["proj"]:
                    continue
                if True:
                    for half in range(2):
                        ts512 = ts * 2 + half
                        p.op("pe", lambda e, half=half:
                             e.matmul(P[0][:, :], ones05[:, :],
                                      r_S[0:1, half * 512:(half + 1) * 512],
                                      start=True, stop=True),
                             r=("r_S", "ones05"), w=("P0",))
                        p.op("act", lambda e:
                             e.activation(out=R_sb, in_=P[0][:, :],
                                          func=AF.Copy),
                             r=("P0",), w=("sgb",))
                        for dg in range(2):
                            for j2 in range(NJ):
                                wslot, hslot = wpk[j2 % 4], hrk[j2 % 4]
                                p.dma("sp", Wp[j2 % 4],
                                      s_wp[(dg * 64 + j2) * 128:
                                           (dg * 64 + j2 + 1) * 128, :], w=(wslot,))
                                p.dma("act", hrd[j2 % 4],
                                      s_h[j2 * 128:(j2 + 1) * 128,
                                          ts512 * 512:(ts512 + 1) * 512],
                                      w=(hslot,))
                                for d in range(8):
                                    p.op("pe", lambda e, j2=j2, d=d:
                                         e.matmul(P[d][:, :],
                                                  Wp[j2 % 4][:, d * 128:(d + 1) * 128],
                                                  hrd[j2 % 4],
                                                  start=(j2 == 0), stop=(j2 == NJ - 1)),
                                         r=(wslot, hslot),
                                         w=((f"P{d}",) if j2 in (0, NJ - 1) else ()),
                                         inc=(MS["pe"] if d == 7 or j2 == NJ - 1
                                              else None))
                            for d in range(8):
                                fo = dg * 8 + d
                                p.op("dve", lambda e, d=d:
                                     e.tensor_tensor(out=prj, in0=P[d][:, :],
                                                     in1=R_sb, op=ALU.mult),
                                     r=(f"P{d}", "sgb"), w=("hst",))
                                p.op("dve", lambda e, fo=fo, ts512=ts512:
                                     e.tensor_tensor(
                                         out=X[fo][:, ts512 * 512:(ts512 + 1) * 512],
                                         in0=X[fo][:, ts512 * 512:(ts512 + 1) * 512],
                                         in1=prj, op=ALU.add),
                                     r=("hst",), w=(f"X{fo}",))
                p.barrier()
        p.barrier()

    def attn():
        with ExitStack() as ax:
            asb = lambda nm, sh, dt: ax.enter_context(nc.sbuf_tensor(nm, sh, dt))
            rcols = asb("at_rcols", [128, 16], F32)
            with ExitStack() as bx:
                bsb = lambda nm, sh, dt: bx.enter_context(nc.sbuf_tensor(nm, sh, dt))
                r_mu = bsb("a_r_mu", [1, T], F32)
                r_A = bsb("a_r_A", [1, T], F32)
                Msb = bsb("a_Msb", [128, 512], F32)
                Rsb = bsb("a_Rsb", [128, 512], F32)
                xn = [bsb(f"a_xn{i}", [128, 1024], BF16) for i in range(NE)]
                ln_rows(r_mu, r_A, "att")
                Wsl = [bsb(f"a_Wsl{i}", [128, 4096], BF16) for i in range(2)]
                Wv = [bsb(f"a_Wv{i}", [128, 2048], BF16) for i in range(2)]
                qst = bsb("a_qst", [128, 512], BF16)
                vst = bsb("a_vst", [128, 512], BF16)
                for ts in range(2):
                    ln_apply(r_mu, r_A, xn, Msb, Rsb, ts)
                    for which, wsrc in ((0, s_wq), (1, s_wk)):
                        wr = wsrc.rearrange("(ec pp) f -> pp ec f", pp=128)
                        for fc in range(4):
                            for hh in range(2):
                                hsl = f"a_Wsl{hh}"
                                p.dma("sp", Wsl[hh][:, :].rearrange(
                                        "pp (ec f) -> pp ec f", ec=8),
                                      wr[:, hh * 8:(hh + 1) * 8,
                                         fc * 512:(fc + 1) * 512],
                                      w=(hsl,))
                            for sl in range(8):
                                pn = f"P{sl % 2}"
                                sc = ts * 8 + sl
                                for ec in range(NE):
                                    hh = ec // 8
                                    p.op("pe", lambda e, sl=sl, ec=ec, hh=hh:
                                         e.matmul(P[sl % 2][:, :],
                                                  xn[ec][:, sl * 128:(sl + 1) * 128],
                                                  Wsl[hh][:, (ec % 8) * 512:
                                                          (ec % 8 + 1) * 512],
                                                  start=(ec == 0), stop=(ec == 15)),
                                         r=(f"a_Wsl{hh}", f"xn{ec}"),
                                         w=((pn,) if ec in (0, 15) else ()),
                                         inc=(MS["pe"] if ec in (7, 15) else None))
                                if which == 0:
                                    p.op("act", lambda e, sl=sl:
                                         e.activation(out=qst[:, :],
                                                      in_=P[sl % 2][:, :],
                                                      func=AF.Copy, scale=0.125),
                                         r=(pn,), w=("a_qst",))
                                    dst = s_q[(sc * 16 + fc * 4) * 128:
                                              (sc * 16 + fc * 4 + 4) * 128,
                                              :].rearrange("(b pp) c -> pp b c",
                                                           pp=128)
                                    p.dma("po", dst,
                                          qst[:, :].rearrange("pp (b c) -> pp b c",
                                                              b=4),
                                          r=("a_qst",), w=("s_q",), sem="d_a_qst")
                                else:
                                    p.op("act", lambda e, sl=sl:
                                         e.activation(out=qst[:, :],
                                                      in_=P[sl % 2][:, :],
                                                      func=AF.Copy),
                                         r=(pn,), w=("a_qst",))
                                    p.dma("po", s_k[sc * 128:(sc + 1) * 128,
                                                    fc * 512:(fc + 1) * 512],
                                          qst[:, :], r=("a_qst",), w=("s_k",),
                                          sem="d_a_qst")
                    for fv in range(NE):
                        wslot = f"a_Wv{fv % 2}"
                        p.dma("sp", Wv[fv % 2][:, :],
                              s_wv2[fv * 128:(fv + 1) * 128, :], w=(wslot,))
                        for nn in range(2):
                            pn = f"P{2 + nn}"
                            for ec in range(NE):
                                p.op("pe", lambda e, fv=fv, nn=nn, ec=ec:
                                     e.matmul(P[2 + nn][:, :],
                                              Wv[fv % 2][:, ec * 128:(ec + 1) * 128],
                                              xn[ec][:, nn * 512:(nn + 1) * 512],
                                              start=(ec == 0), stop=(ec == 15)),
                                     r=(wslot, f"xn{ec}"),
                                     w=((pn,) if ec in (0, 15) else ()),
                                     inc=(MS["pe"] if ec == 15 else None))
                            p.op("act", lambda e, nn=nn:
                                 e.activation(out=vst[:, :], in_=P[2 + nn][:, :],
                                              func=AF.Copy),
                                 r=(pn,), w=("a_vst",))
                            p.dma("po", s_v[fv * 128:(fv + 1) * 128,
                                            ts * 1024 + nn * 512:
                                            ts * 1024 + (nn + 1) * 512],
                                  vst[:, :], r=("a_vst",), w=("s_v",), sem="d_a_vst")
            p.barrier()

            with ExitStack() as sx:
                ssb = lambda nm, sh, dt: sx.enter_context(nc.sbuf_tensor(nm, sh, dt))
                acc = ssb("st_acc", [128, 8], F32)
                srow = ssb("st_srow", [128, 1], F32)
                As = [ssb(f"st_As{i}", [128, T], F32) for i in range(2)]
                kb = [ssb(f"st_kb{i}", [128, T], BF16) for i in range(4)]
                qb = [ssb(f"st_qb{i}", [128, 256], BF16) for i in range(4)]
                ats = [ssb(f"st_ats{i}", [128, 128], BF16) for i in range(2)]
                for pr in range(8):
                    eoA, eoB = 2 * pr, 2 * pr + 1
                    for sc in range(16):
                        kslot, qslot = f"st_kb{sc % 4}", f"st_qb{sc % 4}"
                        p.dma("sp", kb[sc % 4][:, :],
                              s_k[sc * 128:(sc + 1) * 128, :], w=(kslot,))
                        qsrc = s_q[(sc * 16 + eoA) * 128:(sc * 16 + eoB + 1) * 128, :]
                        p.dma("sp",
                              qb[sc % 4][:, :].rearrange("pp (b c) -> pp b c", b=2),
                              qsrc.rearrange("(b pp) c -> pp b c", pp=128),
                              w=(qslot,))
                        for half in range(2):
                            for fc in range(4):
                                bank = half * 4 + fc
                                p.op("pe", lambda e, sc=sc, fc=fc, half=half,
                                     bank=bank:
                                     e.matmul(P[bank][:, :],
                                              qb[sc % 4][:, half * 128:
                                                         half * 128 + 128],
                                              kb[sc % 4][:, fc * 512:(fc + 1) * 512],
                                              start=(sc == 0), stop=(sc == 15)),
                                     r=(kslot, qslot),
                                     w=((f"P{bank}",) if sc in (0, 15) else ()),
                                     inc=(MS["pe"] if (sc == 15 or
                                                       (half, fc) == (1, 3))
                                          else None))
                    for half, eo in ((0, eoA), (1, eoB)):
                        for fc in range(4):
                            bank = half * 4 + fc
                            p.op("act", lambda e, half=half, fc=fc, bank=bank:
                                 e.activation(
                                     out=As[half][:, fc * 512:(fc + 1) * 512],
                                     in_=P[bank][:, :], func=AF.Exp, bias=cb[:, 2:3],
                                     accum_out=acc[:, bank:bank + 1]),
                                 r=(f"P{bank}",), w=(f"st_As{half}", "st_acc"))
                        p.op("dve", lambda e, half=half:
                             e.tensor_reduce(out=srow[:, 0:1],
                                             in_=acc[:, half * 4:half * 4 + 4],
                                             axis=AX.X, op=ALU.add),
                             r=("st_acc",), w=("st_srow",))
                        p.op("dve", lambda e, eo=eo:
                             e.reciprocal(out=rcols[:, eo:eo + 1], in_=srow[:, 0:1]),
                             r=("st_srow",), w=("at_rcols",))
                        for b in range(16):
                            bank = half * 4 + b // 4
                            p.op("pe", lambda e, half=half, b=b, bank=bank:
                                 e.transpose(
                                     P[bank][:, (b % 4) * 128:(b % 4 + 1) * 128],
                                     As[half][:, b * 128:(b + 1) * 128],
                                     eye_sb[:, :]),
                                 r=(f"st_As{half}", "eye_sb"),
                                 w=((f"P{bank}",) if b % 4 == 0 else ()),
                                 inc=(MS["pe"] if b % 4 == 3 else None))
                        for b in range(16):
                            bank = half * 4 + b // 4
                            aslot = f"st_ats{b % 2}"
                            p.op("act", lambda e, b=b, bank=bank:
                                 e.activation(
                                     out=ats[b % 2][:, :],
                                     in_=P[bank][:, (b % 4) * 128:(b % 4 + 1) * 128],
                                     func=AF.Copy),
                                 r=(f"P{bank}",), w=(aslot,))
                            p.dma("po",
                                  s_at[(b * 16 + eo) * 128:(b * 16 + eo + 1) * 128, :],
                                  ats[b % 2][:, :], r=(aslot,), w=("s_at",),
                                  sem="d_" + aslot)
            p.barrier()

            with ExitStack() as wx:
                wsb = lambda nm, sh, dt: wx.enter_context(nc.sbuf_tensor(nm, sh, dt))
                attsb = [wsb(f"w_att{i}", [128, 512], BF16) for i in range(16)]
                ATb = [wsb(f"w_AT{i}", [128, 1024], BF16) for i in range(4)]
                vt = [wsb(f"w_vt{i}", [128, 512], BF16) for i in range(4)]
                Wob = [wsb(f"w_Wo{i}", [128, 2048], BF16) for i in range(3)]
                for ss in range(4):
                    for eg in range(2):
                        for fc in range(16):
                            aslot, vslot = f"w_AT{fc % 4}", f"w_vt{fc % 4}"
                            strip = s_at[(fc * 16 + eg * 8) * 128:
                                         (fc * 16 + eg * 8 + 8) * 128, :]
                            p.dma("sp",
                                  ATb[fc % 4][:, :].rearrange(
                                      "pp (b c) -> pp b c", b=8),
                                  strip.rearrange("(b pp) c -> pp b c", pp=128),
                                  w=(aslot,))
                            p.dma("act", vt[fc % 4][:, :],
                                  s_v[fc * 128:(fc + 1) * 128,
                                      ss * 512:(ss + 1) * 512], w=(vslot,))
                            for d in range(8):
                                p.op("pe", lambda e, fc=fc, d=d:
                                     e.matmul(P[d][:, :],
                                              ATb[fc % 4][:, d * 128:(d + 1) * 128],
                                              vt[fc % 4][:, :],
                                              start=(fc == 0), stop=(fc == 15)),
                                     r=(aslot, vslot),
                                     w=((f"P{d}",) if fc in (0, 15) else ()),
                                     inc=(MS["pe"] if d == 7 or fc == 15 else None))
                        for d in range(8):
                            eo = eg * 8 + d
                            p.op("dve", lambda e, d=d, eo=eo:
                                 e.tensor_scalar(out=attsb[eo][:, :], in0=P[d][:, :],
                                                 scalar1=rcols[:, eo:eo + 1],
                                                 scalar2=None, op0=ALU.mult),
                                 r=(f"P{d}", "at_rcols"), w=(f"w_att{eo}",))
                    for fo in range(16):
                        wslot = f"w_Wo{fo % 3}"
                        p.dma("sp", Wob[fo % 3][:, :],
                              s_wo2[fo * 128:(fo + 1) * 128, :], w=(wslot,))
                        pn = f"P{fo % 2}"
                        for ec in range(NE):
                            p.op("pe", lambda e, fo=fo, ec=ec:
                                 e.matmul(P[fo % 2][:, :],
                                          Wob[fo % 3][:, ec * 128:(ec + 1) * 128],
                                          attsb[ec][:, :],
                                          start=(ec == 0), stop=(ec == 15)),
                                 r=(wslot, f"w_att{ec}"),
                                 w=((pn,) if ec in (0, 15) else ()),
                                 inc=(MS["pe"] if ec == 15 else None))
                        p.op("dve", lambda e, fo=fo, ss=ss:
                             e.scalar_tensor_tensor(
                                 out=X[fo][:, ss * 512:(ss + 1) * 512],
                                 in0=P[fo % 2][:, :], scalar=1.0,
                                 in1=X[fo][:, ss * 512:(ss + 1) * 512],
                                 op0=ALU.mult, op1=ALU.add),
                             r=(pn,), w=(f"X{fo}",))
            p.barrier()

    if "ffn1" in phases:
        ffn_pass("f1")
    elif "quant" in phases:
        pass
    if want_attn:
        attn()
    if "ffn2" in phases:
        ffn_pass("f2")

    for i in range(NE):
        p.dma("po", yT[i * 128:(i + 1) * 128, :], X[i][:, :], r=(f"X{i}",),
              sem="d_out")
    p.emit("po", lambda e: e.nop(), [("d_out", p.cnt["d_out"])], None, 1)

    sems = {s: es.enter_context(nc.semaphore(s)) for s in list(SEMS) + p.dsems}

    def replay(name):
        def run(eng):
            for fn, waits, inc, n in p.streams[name]:
                for s, v in waits:
                    eng.wait_ge(sems[s], v)
                inst = fn(eng)
                if inc is not None:
                    inst.then_inc(sems[inc], n)
        return run

    with nc.Block() as block:
        block.sync(replay("sp"))
        block.tensor(replay("pe"))
        block.scalar(replay("act"))
        block.vector(replay("dve"))
        block.gpsimd(replay("po"))

    es.close()
    build.last_prog = p
    return nc


_BUILD_CACHE = {}


def _get_nc(phases):
    key = tuple(phases)
    if key not in _BUILD_CACHE:
        _BUILD_CACHE[key] = build(phases)
    return _BUILD_CACHE[key]


def kernel(L0_attn_x, L0_attn_Wq, L0_attn_Wk, L0_attn_Wv, L0_attn_Wo,
           L0_ffn_w_gate_up, L0_ffn_w_proj):
    x = np.asarray(L0_attn_x, dtype=np.float32)
    wq = np.ascontiguousarray(np.asarray(L0_attn_Wq, dtype=np.float32))
    wk = np.ascontiguousarray(np.asarray(L0_attn_Wk, dtype=np.float32))
    wv = np.ascontiguousarray(np.asarray(L0_attn_Wv, dtype=np.float32))
    wo = np.ascontiguousarray(np.asarray(L0_attn_Wo, dtype=np.float32))
    wguT = np.ascontiguousarray(np.asarray(L0_ffn_w_gate_up, dtype=np.float32).T)
    wpT = np.ascontiguousarray(np.asarray(L0_ffn_w_proj, dtype=np.float32).T)
    ind16, sel16, eye = _consts_np()

    nc = _get_nc(PHASES)
    in_maps = []
    for i in range(8):
        in_maps.append({
            "xT": np.ascontiguousarray(x[i].T),
            "wq": wq, "wk": wk, "wv": wv, "wo": wo,
            "wguT": wguT, "wpT": wpT,
            "eye_f": eye, "ind16": ind16, "sel16": sel16,
        })
    res = run_bass_kernel_spmd(nc, in_maps, core_ids=list(range(8)))
    out = np.empty((8, T, D), dtype=np.float32)
    for i in range(8):
        out[i] = res.results[i]["yT"].T
    return out


# version 03 (final)
# speedup vs baseline: 1.3734x; 1.1077x over previous
import numpy as np
import ml_dtypes
from contextlib import ExitStack

import concourse.bass as bass
import concourse.mybir as mybir
from concourse.bass_utils import run_bass_kernel_spmd

F32 = mybir.dt.float32
BF16 = mybir.dt.bfloat16
AF = mybir.ActivationFunctionType
ALU = mybir.AluOpType
AX = mybir.AxisListType
BF = ml_dtypes.bfloat16

D = 2048
T = 2048
NE = 16
H = 8192
NJ = 64
THRC = float(0.5 + 2.0 ** -10)
LN_EPS = 1e-5
RMS_EPS = float(np.finfo(np.float32).eps)
EXP_BIAS = -15.0

PHASES = ("ffn1", "attn", "ffn2")
SUB = {"ln": True, "apply": True, "gu": True, "proj": True, "nj": NJ, "silu": True, "up": True, "hmul": True, "sq": True, "stats": True, "hdma": True, "rms": True, "silu_fn": "Silu", "noquant": False}

ENGS = ("sp", "pe", "act", "dve", "po")
MS = {"sp": "msp", "pe": "mpe", "act": "mact", "dve": "mdve", "po": "mpo"}
SEMS = ("msp", "mpe", "mact", "mdve", "mpo")


class Buf:
    __slots__ = ("w", "r")

    def __init__(self):
        self.w = None
        self.r = {}


class Prog:
    def __init__(self):
        self.streams = {e: [] for e in ENGS}
        self.cnt = {s: 0 for s in SEMS}
        self.dsems = []
        self.ewait = {e: {} for e in ENGS}
        self.bufs = {}
        self.pend = {e: [] for e in ENGS}

    def dsem(self, name):
        if name not in self.cnt:
            self.cnt[name] = 0
            self.dsems.append(name)
        return name

    def fence_now(self):
        f = dict(getattr(self, "fence", ()))
        for b in self.bufs.values():
            if b.w:
                s, v = b.w
                f[s] = max(f.get(s, 0), v)
            for s, v in b.r.items():
                f[s] = max(f.get(s, 0), v)
        self.fence = tuple(f.items())

    def buf(self, name):
        b = self.bufs.get(name)
        if b is None:
            b = self.bufs[name] = Buf()
        return b

    def emit(self, eng, fn, waits, inc, n):
        wl = {}
        for s, v in waits:
            if v > 0:
                wl[s] = max(wl.get(s, 0), v)
        eh = self.ewait[eng]
        final = []
        for s, v in wl.items():
            if v > eh.get(s, 0):
                assert v <= self.cnt[s], \
                    f"{eng}: wait {s}>={v} unreachable (emitted {self.cnt[s]})"
                final.append((s, v))
                eh[s] = v
        after = None
        if inc is not None:
            self.cnt[inc] += n
            after = self.cnt[inc]
        self.streams[eng].append((fn, tuple(final), inc, n))
        return after

    def op(self, eng, fn, r=(), w=(), inc="default", n=1):
        waits = []
        for name in r:
            b = self.buf(name)
            if b.w:
                waits.append(b.w)
        for name in w:
            b = self.buf(name)
            if b.w:
                waits.append(b.w)
            waits.extend(b.r.items())
            if b.w is None and not b.r:
                waits.extend(getattr(self, "fence", ()))
        if inc == "default":
            inc = MS[eng]
        after = self.emit(eng, fn, waits, inc, n)
        if inc is None:
            self.pend[eng].append((tuple(r), tuple(w)))
        else:
            for (pr, pw) in self.pend[eng] + [(tuple(r), tuple(w))]:
                for name in pr:
                    b = self.buf(name)
                    b.r[inc] = max(b.r.get(inc, 0), after)
                for name in pw:
                    b = self.buf(name)
                    b.w = (inc, after)
                    b.r = {}
            self.pend[eng] = []
        return after

    def dma(self, q, dst, src, r=(), w=(), sem=None):
        if sem is None:
            assert w, "dma needs w= slot or explicit sem"
            sem = "d_" + w[0]
        return self.op(q, lambda e, a=dst, b=src: e.dma_start(out=a, in_=b),
                       r=r, w=w, inc=self.dsem(sem), n=16)

    def barrier(self):
        for eng in ENGS:
            assert not self.pend[eng], f"pending notes on {eng} at barrier"
        waits = [(MS[o], self.cnt[MS[o]]) for o in ENGS]
        waits += [(s, self.cnt[s]) for s in self.dsems]
        for eng in ENGS:
            self.emit(eng, lambda e: e.nop(),
                      [wv for wv in waits if wv[0] != MS[eng]], None, 1)
        for b in self.bufs.values():
            b.w = None
            b.r = {}
        self.fence = ()


def _consts_np():
    ind16 = np.zeros((128, 512), dtype=BF)
    sel16 = np.zeros((32, 2048), dtype=BF)
    for ec in range(16):
        ind16[0:64, ec * 32 + 2 * ec] = 1
        ind16[64:128, ec * 32 + 2 * ec + 1] = 1
        sel16[2 * ec, ec * 128: ec * 128 + 64] = 1
        sel16[2 * ec + 1, ec * 128 + 64: ec * 128 + 128] = 1
    eye = np.eye(128, dtype=np.float32)
    return ind16, sel16, eye


def build(phases=None):
    if phases is None:
        phases = PHASES
    want_ffn = ("ffn1" in phases) or ("ffn2" in phases) or ("quant" in phases)
    want_attn = "attn" in phases

    nc = bass.Bass("TRN2", target_bir_lowering=False)
    p = Prog()

    xT = nc.dram_tensor("xT", [D, T], F32, kind="ExternalInput").ap()
    yT = nc.dram_tensor("yT", [D, T], F32, kind="ExternalOutput").ap()
    wq_d = nc.dram_tensor("wq", [D, D], F32, kind="ExternalInput").ap()
    wk_d = nc.dram_tensor("wk", [D, D], F32, kind="ExternalInput").ap()
    wv_d = nc.dram_tensor("wv", [D, D], F32, kind="ExternalInput").ap()
    wo_d = nc.dram_tensor("wo", [D, D], F32, kind="ExternalInput").ap()
    wguT = nc.dram_tensor("wguT", [D, 2 * H], F32, kind="ExternalInput").ap()
    wpT = nc.dram_tensor("wpT", [H, D], F32, kind="ExternalInput").ap()
    eye_d = nc.dram_tensor("eye_f", [128, 128], F32, kind="ExternalInput").ap()
    ind16_d = nc.dram_tensor("ind16", [128, 512], BF16, kind="ExternalInput").ap()
    sel16_d = nc.dram_tensor("sel16", [32, 2048], BF16, kind="ExternalInput").ap()

    s_wgu = nc.dram_tensor("s_wgu", [128 * 128, 2048], BF16, kind="Internal").ap()
    s_wp = nc.dram_tensor("s_wp", [128 * 128, 1024], BF16, kind="Internal").ap()
    s_h = nc.dram_tensor("s_h", [H, T], BF16, kind="Internal").ap()
    s_wq = nc.dram_tensor("s_wq", [D, D], BF16, kind="Internal").ap()
    s_wk = nc.dram_tensor("s_wk", [D, D], BF16, kind="Internal").ap()
    s_wv2 = nc.dram_tensor("s_wv2", [D, D], BF16, kind="Internal").ap()
    s_wo2 = nc.dram_tensor("s_wo2", [D, D], BF16, kind="Internal").ap()
    s_q = nc.dram_tensor("s_q", [16 * 16 * 128, 128], BF16, kind="Internal").ap()
    s_k = nc.dram_tensor("s_k", [T, D], BF16, kind="Internal").ap()
    s_v = nc.dram_tensor("s_v", [D, T], BF16, kind="Internal").ap()
    s_at = nc.dram_tensor("s_at", [16 * 16 * 128, 128], BF16, kind="Internal").ap()

    es = ExitStack()

    def sb(name, shape, dt):
        return es.enter_context(nc.sbuf_tensor(name, shape, dt))

    def ps(name, shape, dt):
        return es.enter_context(nc.psum_tensor(name, shape, dt))

    X = [sb(f"X{i}", [128, T], F32) for i in range(NE)]
    onesK1 = sb("onesK1", [1, 128], F32)
    ones05 = sb("ones05", [1, 128], F32)
    ones_bf = sb("ones_bf", [128, 1], BF16)
    eye_sb = sb("eye_sb", [128, 128], F32)
    cb = sb("cb", [128, 3], F32)

    P = [ps(f"P{i}", [128, 512], F32) for i in range(8)]

    p.op("dve", lambda e: e.memset(onesK1[:, :], 1.0), w=("onesK1",))
    p.op("dve", lambda e: e.memset(ones05[:, :], 0.5), w=("ones05",))
    p.op("dve", lambda e: e.memset(ones_bf[:, :], 1.0), w=("ones_bf",))
    p.op("dve", lambda e: e.memset(cb[:, 0:1], LN_EPS), w=("cb",))
    p.op("dve", lambda e: e.memset(cb[:, 1:2], RMS_EPS), w=("cb",))
    p.op("dve", lambda e: e.memset(cb[:, 2:3], EXP_BIAS), w=("cb",))
    p.dma("sp", eye_sb[:, :], eye_d, w=("eye_sb",))

    for i in range(NE):
        p.dma("sp", X[i][:, :], xT[i * 128:(i + 1) * 128, :], w=(f"X{i}",),
              sem="d_xload")
    for i in range(NE):
        p.bufs[f"X{i}"].w = ("d_xload", p.cnt["d_xload"])

    if want_attn:
        p.dma("po", s_wq, wq_d, sem="d_cast")
        p.dma("po", s_wk, wk_d, sem="d_cast")
        wv_r = wv_d.rearrange("(ec pp) (fo c) -> fo pp ec c", pp=128, c=128)
        wo_r = wo_d.rearrange("(ec pp) (fo c) -> fo pp ec c", pp=128, c=128)
        sv_r = s_wv2.rearrange("(fo pp) (ec c) -> fo pp ec c", pp=128, c=128)
        so_r = s_wo2.rearrange("(fo pp) (ec c) -> fo pp ec c", pp=128, c=128)
        for fo in range(16):
            p.dma("po", sv_r[fo], wv_r[fo], sem="d_cast")
            p.dma("po", so_r[fo], wo_r[fo], sem="d_cast")
        for nmx in ("s_wq", "s_wk", "s_wv2", "s_wo2"):
            p.buf(nmx).w = ("d_cast", p.cnt["d_cast"])

    if want_ffn and not SUB["noquant"]:
        with ExitStack() as qx:
            qsb = lambda nm, sh, dt: qx.enter_context(nc.sbuf_tensor(nm, sh, dt))
            ind16 = qsb("ind16s", [128, 512], BF16)
            sel16 = qsb("sel16s", [32, 2048], BF16)
            wt = [qsb(f"q_wt{i}", [128, 1024], BF16) for i in range(16)]
            aw = [qsb(f"q_aw{i}", [128, 1024], BF16) for i in range(2)]
            s_bf = qsb("q_sbf", [32, 1024], BF16)
            S_sb = [qsb(f"q_Ssb{i}", [128, 1024], BF16) for i in range(4)]
            m1 = [qsb(f"q_m1{i}", [128, 1024], BF16) for i in range(4)]
            m2 = [qsb(f"q_m2{i}", [128, 1024], BF16) for i in range(4)]
            Tp = [qsb(f"q_T{i}", [128, 1024], BF16) for i in range(2)]
            wh = [qsb(f"q_wh{i}", [128, 1024], BF16) for i in range(3)]

            p.dma("sp", ind16[:, :], ind16_d, w=("ind16",))
            p.dma("sp", sel16[:, :], sel16_d, w=("sel16",))

            jobs = [("gu", st) for st in range(16)] + [("wp", st) for st in range(8)]
            for kind, st in jobs:
                for ec in range(16):
                    wslot = f"q_wt{ec}"
                    if kind == "gu":
                        src = wguT[ec * 128:(ec + 1) * 128, st * 1024:(st + 1) * 1024]
                    else:
                        hg, dgq = st // 2, st % 2
                        hq = hg * 16 + ec
                        src = wpT[hq * 128:(hq + 1) * 128, dgq * 1024:(dgq + 1) * 1024]
                    p.dma("po", wt[ec][:, :], src, w=(wslot,))
                    p.op("act", lambda e, ec=ec:
                         e.activation(out=aw[ec % 2][:, :], in_=wt[ec][:, :],
                                      func=AF.Abs),
                         r=(wslot,), w=(f"q_aw{ec % 2}",))
                    for nn in range(2):
                        p.op("pe", lambda e, ec=ec, nn=nn:
                             e.matmul(P[nn][0:32, :], ind16[:, ec * 32:(ec + 1) * 32],
                                      aw[ec % 2][:, nn * 512:(nn + 1) * 512],
                                      start=(ec == 0), stop=(ec == 15)),
                             r=(f"q_aw{ec % 2}", "ind16"),
                             w=((f"P{nn}",) if ec in (0, 15) else ()),
                             inc=(MS["pe"] if nn == 1 else None))
                for nn in range(2):
                    p.op("dve", lambda e, nn=nn:
                         e.tensor_scalar(out=s_bf[:, nn * 512:(nn + 1) * 512],
                                         in0=P[nn][0:32, :], scalar1=1.0 / 64,
                                         scalar2=1e-8, op0=ALU.mult, op1=ALU.max),
                         r=(f"P{nn}",), w=("q_sbf",))
                for ec in range(16):
                    sslot, mslot, nslot, whs = (f"q_Ssb{ec % 4}", f"q_m1{ec % 4}",
                                                f"q_m2{ec % 4}", f"q_wh{ec % 3}")
                    p.op("pe", lambda e, ec=ec:
                         e.matmul(P[2 + ec % 2][:, :],
                                  sel16[:, ec * 128:(ec + 1) * 128],
                                  s_bf[:, 0:512], start=True, stop=True),
                         r=("q_sbf", "sel16"), w=(f"P{2 + ec % 2}",))
                    p.op("act", lambda e, ec=ec:
                         e.activation(out=S_sb[ec % 4][:, 0:512],
                                      in_=P[2 + ec % 2][:, :], func=AF.Copy),
                         r=(f"P{2 + ec % 2}",), w=(sslot,))
                    p.op("pe", lambda e, ec=ec:
                         e.matmul(P[4 + ec % 2][:, :],
                                  sel16[:, ec * 128:(ec + 1) * 128],
                                  s_bf[:, 512:1024], start=True, stop=True),
                         r=("q_sbf", "sel16"), w=(f"P{4 + ec % 2}",))
                    p.op("act", lambda e, ec=ec:
                         e.activation(out=S_sb[ec % 4][:, 512:1024],
                                      in_=P[4 + ec % 2][:, :], func=AF.Copy),
                         r=(f"P{4 + ec % 2}",), w=(sslot,))
                    tslot = f"q_T{ec % 2}"
                    p.op("dve", lambda e, ec=ec:
                         e.tensor_scalar(out=Tp[ec % 2][:, :], in0=S_sb[ec % 4][:, :],
                                         scalar1=THRC, scalar2=None, op0=ALU.mult),
                         r=(sslot,), w=(tslot,))
                    p.op("dve", lambda e, ec=ec:
                         e.tensor_tensor(out=m1[ec % 4][:, :], in0=Tp[ec % 2][:, :],
                                         in1=wt[ec][:, :], op=ALU.is_lt),
                         r=(tslot, f"q_wt{ec}"), w=(mslot,))
                    p.op("dve", lambda e, ec=ec:
                         e.tensor_scalar(out=Tp[ec % 2][:, :], in0=S_sb[ec % 4][:, :],
                                         scalar1=-THRC, scalar2=None, op0=ALU.mult),
                         r=(sslot,), w=(tslot,))
                    p.op("dve", lambda e, ec=ec:
                         e.tensor_tensor(out=m2[ec % 4][:, :], in0=Tp[ec % 2][:, :],
                                         in1=wt[ec][:, :], op=ALU.is_gt),
                         r=(tslot, f"q_wt{ec}"), w=(nslot,))
                    p.op("dve", lambda e, ec=ec:
                         e.tensor_tensor(out=m1[ec % 4][:, :], in0=m1[ec % 4][:, :],
                                         in1=m2[ec % 4][:, :], op=ALU.subtract),
                         r=(nslot,), w=(mslot,))
                    p.op("dve", lambda e, ec=ec:
                         e.tensor_tensor(out=wh[ec % 3][:, :], in0=m1[ec % 4][:, :],
                                         in1=S_sb[ec % 4][:, :], op=ALU.mult),
                         r=(mslot, sslot), w=(whs,))
                    if kind == "gu":
                        j0 = st * 8
                        dst = s_wgu[j0 * 128:(j0 + 8) * 128,
                                    ec * 128:(ec + 1) * 128].rearrange(
                                        "(j pp) c -> pp j c", pp=128)
                        srcap = wh[ec % 3][:, :].rearrange("pp (j c) -> pp j c", j=8)
                        p.dma("po", dst, srcap, r=(whs,), w=("s_wgu",),
                              sem="d_" + whs)
                    else:
                        hg, dgq = st // 2, st % 2
                        hq = hg * 16 + ec
                        row0 = (dgq * 64 + hq) * 128
                        p.dma("po", s_wp[row0:row0 + 128, :], wh[ec % 3][:, :],
                              r=(whs,), w=("s_wp",), sem="d_" + whs)
        p.barrier()

    def ln_rows(r_mu, r_A, tag):
        with ExitStack() as lx:
            lsb = lambda nm, sh, dt: lx.enter_context(nc.sbuf_tensor(nm, sh, dt))
            xbf = [lsb(f"xbf{i}_{tag}", [128, T], BF16) for i in range(2)]
            sq = [lsb(f"sq{i}_{tag}", [128, T], BF16) for i in range(2)]
            for ec in range(NE):
                xs, qs = f"lnx{ec % 2}", f"lnq{ec % 2}"
                p.op("act", lambda e, ec=ec: e.activation(
                    out=xbf[ec % 2][:, :], in_=X[ec][:, :], func=AF.Copy),
                    r=(f"X{ec}",), w=(xs,))
                p.op("act", lambda e, ec=ec: e.activation(
                    out=sq[ec % 2][:, :], in_=xbf[ec % 2][:, :], func=AF.Square),
                    r=(xs,), w=(qs,))
                for nn in range(4):
                    p.op("pe", lambda e, ec=ec, nn=nn:
                         e.matmul(P[4 + nn][0:1, :], ones_bf[:, :],
                                  xbf[ec % 2][:, nn * 512:(nn + 1) * 512],
                                  start=(ec == 0), stop=(ec == 15)),
                         r=(xs, "ones_bf"),
                         w=((f"P{4 + nn}",) if ec in (0, 15) else ()),
                         inc=None)
                for nn in range(4):
                    p.op("pe", lambda e, ec=ec, nn=nn:
                         e.matmul(P[nn][0:1, :], ones_bf[:, :],
                                  sq[ec % 2][:, nn * 512:(nn + 1) * 512],
                                  start=(ec == 0), stop=(ec == 15)),
                         r=(qs, "ones_bf"),
                         w=((f"P{nn}",) if ec in (0, 15) else ()),
                         inc=(MS["pe"] if nn == 3 else None))
        p.fence_now()
        for nn in range(4):
            p.op("act", lambda e, nn=nn:
                 e.activation(out=r_mu[0:1, nn * 512:(nn + 1) * 512],
                              in_=P[4 + nn][0:1, :], func=AF.Copy, scale=1.0 / D),
                 r=(f"P{4 + nn}",), w=("r_mu",))
        p.op("dve", lambda e:
             e.tensor_tensor(out=r_A[0:1, :], in0=r_mu[0:1, :], in1=r_mu[0:1, :],
                             op=ALU.mult),
             r=("r_mu",), w=("r_A",))
        for nn in range(4):
            p.op("dve", lambda e, nn=nn:
                 e.scalar_tensor_tensor(out=r_A[0:1, nn * 512:(nn + 1) * 512],
                                        in0=P[nn][0:1, :], scalar=1.0 / D,
                                        in1=r_A[0:1, nn * 512:(nn + 1) * 512],
                                        op0=ALU.mult, op1=ALU.subtract),
                 r=(f"P{nn}",), w=("r_A",))
        for nn in range(4):
            p.op("act", lambda e, nn=nn:
                 e.activation(out=P[nn][0:1, :],
                              in_=r_A[0:1, nn * 512:(nn + 1) * 512],
                              func=AF.Sqrt, bias=cb[0:1, 0:1]),
                 r=("r_A", "cb"), w=(f"P{nn}",))
        for nn in range(4):
            p.op("dve", lambda e, nn=nn:
                 e.reciprocal(out=r_A[0:1, nn * 512:(nn + 1) * 512],
                              in_=P[nn][0:1, :]),
                 r=(f"P{nn}",), w=("r_A",))

    def ln_apply(r_mu, r_A, xn, Msb, Rsb, ts):
        for hf in range(2):
            base = ts * 1024 + hf * 512
            p.op("pe", lambda e, base=base:
                 e.matmul(P[6][:, :], onesK1[:, :], r_mu[0:1, base:base + 512],
                          start=True, stop=True),
                 r=("r_mu", "onesK1"), w=("P6",))
            p.op("pe", lambda e, base=base:
                 e.matmul(P[7][:, :], onesK1[:, :], r_A[0:1, base:base + 512],
                          start=True, stop=True),
                 r=("r_A", "onesK1"), w=("P7",))
            p.op("act", lambda e:
                 e.activation(out=Msb[:, :], in_=P[6][:, :], func=AF.Copy),
                 r=("P6",), w=("Msb",))
            p.op("act", lambda e:
                 e.activation(out=Rsb[:, :], in_=P[7][:, :], func=AF.Copy),
                 r=("P7",), w=("Rsb",))
            for ec in range(NE):
                sl = slice(hf * 512, hf * 512 + 512)
                p.op("dve", lambda e, ec=ec, sl=sl, base=base:
                     e.scalar_tensor_tensor(out=xn[ec][:, sl],
                                            in0=X[ec][:, base:base + 512], scalar=1.0,
                                            in1=Msb[:, :], op0=ALU.mult,
                                            op1=ALU.subtract),
                     r=(f"X{ec}", "Msb"), w=(f"xn{ec}",))
                p.op("dve", lambda e, ec=ec, sl=sl:
                     e.tensor_tensor(out=xn[ec][:, sl], in0=xn[ec][:, sl],
                                     in1=Rsb[:, :], op=ALU.mult),
                     r=("Rsb",), w=(f"xn{ec}",))

    def ffn_pass(tag):
        with ExitStack() as fx:
            fsb = lambda nm, sh, dt: fx.enter_context(
                nc.sbuf_tensor(nm + "_" + tag, sh, dt))
            r_mu = fsb("r_mu", [1, T], F32)
            r_A = fsb("r_A", [1, T], F32)
            r_S = fsb("r_S", [1, 1024], F32)
            Msb = fsb("Msb", [128, 512], F32)
            Rsb = fsb("Rsb", [128, 512], F32)
            xn = [fsb(f"xn{i}", [128, 1024], BF16) for i in range(NE)]

            if SUB["ln"]:
                ln_rows(r_mu, r_A, tag)
            Wg = [fsb(f"Wg{i}", [128, 2048], BF16) for i in range(2)]
            Wu = [fsb(f"Wu{i}", [128, 2048], BF16) for i in range(2)]
            sgb = fsb("sgb", [128, 1024], BF16)
            hst = fsb("hst", [128, 1024], BF16)
            h2b = [fsb("h2b0", [128, 1024], BF16), Msb[:, :].bitcast(BF16)]
            h2k = ("h2b0", "Msb")
            Wp = [Wg[0][:, 0:1024], Wg[0][:, 1024:2048],
                  Wg[1][:, 0:1024], Wg[1][:, 1024:2048]]
            wpk = ("Wg0a", "Wg0b", "Wg1a", "Wg1b")
            hrd = [Wu[0][:, 0:512], Wu[0][:, 512:1024],
                   Wu[0][:, 1024:1536], Wu[0][:, 1536:2048]]
            hrk = ("Wu0a", "Wu0b", "Wu0c", "Wu0d")
            R_sb = sgb[:, :].bitcast(F32)
            prj = hst[:, :].bitcast(F32)
            for ts in range(2):
                if not SUB["apply"]:
                    continue
                if SUB["ln"]:
                    ln_apply(r_mu, r_A, xn, Msb, Rsb, ts)
                if not SUB["gu"]:
                    continue
                if True:
                    for j in range(SUB["nj"]):
                        gslot, uslot = f"Wg{j % 2}", f"Wu{j % 2}"
                        pgn = ("P0", "P1") if j % 2 == 0 else ("P2", "P3")
                        pg = (P[0], P[1]) if j % 2 == 0 else (P[2], P[3])
                        p.dma("sp", Wg[j % 2][:, :],
                              s_wgu[j * 128:(j + 1) * 128, :], w=(gslot,))
                        p.dma("sp", Wu[j % 2][:, :],
                              s_wgu[(64 + j) * 128:(64 + j + 1) * 128, :], w=(uslot,))
                        for nn in range(2):
                            for ec in range(NE):
                                p.op("pe", lambda e, nn=nn, ec=ec, pg=pg, j=j:
                                     e.matmul(pg[nn][:, :],
                                              Wg[j % 2][:, ec * 128:(ec + 1) * 128],
                                              xn[ec][:, nn * 512:(nn + 1) * 512],
                                              start=(ec == 0), stop=(ec == 15)),
                                     r=(gslot, f"xn{ec}"),
                                     w=((pgn[nn],) if ec in (0, 15) else ()),
                                     inc=(MS["pe"] if ec == 15 else None))
                        if SUB["silu"]:
                          for nn in range(2):
                            p.op("act", lambda e, nn=nn, pg=pg:
                                 e.activation(out=sgb[:, nn * 512:(nn + 1) * 512],
                                              in_=pg[nn][:, :],
                                              func=getattr(AF, SUB["silu_fn"])),
                                 r=(pgn[nn],), w=("sgb",))
                        if not SUB["up"]:
                            continue
                        for nn in range(2):
                            for ec in range(NE):
                                p.op("pe", lambda e, nn=nn, ec=ec, j=j:
                                     e.matmul(P[4 + nn][:, :],
                                              Wu[j % 2][:, ec * 128:(ec + 1) * 128],
                                              xn[ec][:, nn * 512:(nn + 1) * 512],
                                              start=(ec == 0), stop=(ec == 15)),
                                     r=(uslot, f"xn{ec}"),
                                     w=((f"P{4 + nn}",) if ec in (0, 15) else ()),
                                     inc=(MS["pe"] if ec == 15 else None))
                        if not SUB["hmul"]:
                            continue
                        for nn in range(2):
                            p.op("dve", lambda e, nn=nn:
                                 e.tensor_tensor(out=hst[:, nn * 512:(nn + 1) * 512],
                                                 in0=sgb[:, nn * 512:(nn + 1) * 512],
                                                 in1=P[4 + nn][:, :], op=ALU.mult),
                                 r=("sgb", f"P{4 + nn}"), w=("hst",))
                        if not SUB["sq"]:
                            continue
                        p.op("act", lambda e, j=j:
                             e.activation(out=h2b[j % 2][:, :], in_=hst[:, :],
                                          func=AF.Square),
                             r=("hst",), w=(h2k[j % 2],))
                        if not SUB["stats"]:
                            continue
                        def emit_stats(jj):
                            for nn in range(2):
                                p.op("pe", lambda e, nn=nn, jj=jj:
                                     e.matmul(P[6 + nn][0:1, :], ones_bf[:, :],
                                              h2b[jj % 2][:, nn * 512:(nn + 1) * 512],
                                              start=(jj == 0),
                                              stop=(jj == SUB["nj"] - 1)),
                                     r=(h2k[jj % 2], "ones_bf"),
                                     w=((f"P{6 + nn}",)
                                        if jj in (0, SUB["nj"] - 1) else ()),
                                     inc=MS["pe"])
                        if j > 0:
                            emit_stats(j - 1)
                        if j == SUB["nj"] - 1:
                            emit_stats(j)
                        if SUB["hdma"]:
                            p.dma("po",
                                  s_h[j * 128:(j + 1) * 128, ts * 1024:(ts + 1) * 1024],
                                  hst[:, :], r=("hst",), w=("s_h",), sem="d_hst")
                    for nn in range(2 if SUB["rms"] else 0):
                        p.op("act", lambda e, nn=nn:
                             e.activation(out=P[4 + nn][0:1, :], in_=P[6 + nn][0:1, :],
                                          func=AF.Sqrt, scale=1.0 / H, bias=cb[0:1, 1:2]),
                             r=(f"P{6 + nn}", "cb"), w=(f"P{4 + nn}",))
                        p.op("dve", lambda e, nn=nn:
                             e.reciprocal(out=r_S[0:1, nn * 512:(nn + 1) * 512],
                                          in_=P[4 + nn][0:1, :]),
                             r=(f"P{4 + nn}",), w=("r_S",))
                p.barrier()
                if not SUB["proj"]:
                    continue
                if True:
                    for half in range(2):
                        ts512 = ts * 2 + half
                        p.op("pe", lambda e, half=half:
                             e.matmul(P[0][:, :], ones05[:, :],
                                      r_S[0:1, half * 512:(half + 1) * 512],
                                      start=True, stop=True),
                             r=("r_S", "ones05"), w=("P0",))
                        p.op("act", lambda e:
                             e.activation(out=R_sb, in_=P[0][:, :],
                                          func=AF.Copy),
                             r=("P0",), w=("sgb",))
                        for dg in range(2):
                            for j2 in range(NJ):
                                wslot, hslot = wpk[j2 % 4], hrk[j2 % 4]
                                p.dma("sp", Wp[j2 % 4],
                                      s_wp[(dg * 64 + j2) * 128:
                                           (dg * 64 + j2 + 1) * 128, :], w=(wslot,))
                                p.dma("act", hrd[j2 % 4],
                                      s_h[j2 * 128:(j2 + 1) * 128,
                                          ts512 * 512:(ts512 + 1) * 512],
                                      w=(hslot,))
                                for d in range(8):
                                    p.op("pe", lambda e, j2=j2, d=d:
                                         e.matmul(P[d][:, :],
                                                  Wp[j2 % 4][:, d * 128:(d + 1) * 128],
                                                  hrd[j2 % 4],
                                                  start=(j2 == 0), stop=(j2 == NJ - 1)),
                                         r=(wslot, hslot),
                                         w=((f"P{d}",) if j2 in (0, NJ - 1) else ()),
                                         inc=(MS["pe"] if d == 7 or j2 == NJ - 1
                                              else None))
                            for d in range(8):
                                fo = dg * 8 + d
                                p.op("dve", lambda e, d=d:
                                     e.tensor_tensor(out=prj, in0=P[d][:, :],
                                                     in1=R_sb, op=ALU.mult),
                                     r=(f"P{d}", "sgb"), w=("hst",))
                                p.op("dve", lambda e, fo=fo, ts512=ts512:
                                     e.tensor_tensor(
                                         out=X[fo][:, ts512 * 512:(ts512 + 1) * 512],
                                         in0=X[fo][:, ts512 * 512:(ts512 + 1) * 512],
                                         in1=prj, op=ALU.add),
                                     r=("hst",), w=(f"X{fo}",))
                p.barrier()
        p.barrier()

    def attn():
        with ExitStack() as ax:
            asb = lambda nm, sh, dt: ax.enter_context(nc.sbuf_tensor(nm, sh, dt))
            rcols = asb("at_rcols", [128, 16], F32)
            with ExitStack() as bx:
                bsb = lambda nm, sh, dt: bx.enter_context(nc.sbuf_tensor(nm, sh, dt))
                r_mu = bsb("a_r_mu", [1, T], F32)
                r_A = bsb("a_r_A", [1, T], F32)
                Msb = bsb("a_Msb", [128, 512], F32)
                Rsb = bsb("a_Rsb", [128, 512], F32)
                xn = [bsb(f"a_xn{i}", [128, 1024], BF16) for i in range(NE)]
                ln_rows(r_mu, r_A, "att")
                Wsl = [bsb(f"a_Wsl{i}", [128, 4096], BF16) for i in range(2)]
                Wv = [bsb(f"a_Wv{i}", [128, 2048], BF16) for i in range(2)]
                qst = bsb("a_qst", [128, 512], BF16)
                vst = bsb("a_vst", [128, 512], BF16)
                for ts in range(2):
                    ln_apply(r_mu, r_A, xn, Msb, Rsb, ts)
                    for which, wsrc in ((0, s_wq), (1, s_wk)):
                        wr = wsrc.rearrange("(ec pp) f -> pp ec f", pp=128)
                        for fc in range(4):
                            for hh in range(2):
                                hsl = f"a_Wsl{hh}"
                                p.dma("sp", Wsl[hh][:, :].rearrange(
                                        "pp (ec f) -> pp ec f", ec=8),
                                      wr[:, hh * 8:(hh + 1) * 8,
                                         fc * 512:(fc + 1) * 512],
                                      w=(hsl,))
                            for sl in range(8):
                                pn = f"P{sl % 2}"
                                sc = ts * 8 + sl
                                for ec in range(NE):
                                    hh = ec // 8
                                    p.op("pe", lambda e, sl=sl, ec=ec, hh=hh:
                                         e.matmul(P[sl % 2][:, :],
                                                  xn[ec][:, sl * 128:(sl + 1) * 128],
                                                  Wsl[hh][:, (ec % 8) * 512:
                                                          (ec % 8 + 1) * 512],
                                                  start=(ec == 0), stop=(ec == 15)),
                                         r=(f"a_Wsl{hh}", f"xn{ec}"),
                                         w=((pn,) if ec in (0, 15) else ()),
                                         inc=(MS["pe"] if ec in (7, 15) else None))
                                if which == 0:
                                    p.op("act", lambda e, sl=sl:
                                         e.activation(out=qst[:, :],
                                                      in_=P[sl % 2][:, :],
                                                      func=AF.Copy, scale=0.125),
                                         r=(pn,), w=("a_qst",))
                                    dst = s_q[(sc * 16 + fc * 4) * 128:
                                              (sc * 16 + fc * 4 + 4) * 128,
                                              :].rearrange("(b pp) c -> pp b c",
                                                           pp=128)
                                    p.dma("po", dst,
                                          qst[:, :].rearrange("pp (b c) -> pp b c",
                                                              b=4),
                                          r=("a_qst",), w=("s_q",), sem="d_a_qst")
                                else:
                                    p.op("act", lambda e, sl=sl:
                                         e.activation(out=qst[:, :],
                                                      in_=P[sl % 2][:, :],
                                                      func=AF.Copy),
                                         r=(pn,), w=("a_qst",))
                                    p.dma("po", s_k[sc * 128:(sc + 1) * 128,
                                                    fc * 512:(fc + 1) * 512],
                                          qst[:, :], r=("a_qst",), w=("s_k",),
                                          sem="d_a_qst")
                    for fv in range(NE):
                        wslot = f"a_Wv{fv % 2}"
                        p.dma("sp", Wv[fv % 2][:, :],
                              s_wv2[fv * 128:(fv + 1) * 128, :], w=(wslot,))
                        for nn in range(2):
                            pn = f"P{2 + nn}"
                            for ec in range(NE):
                                p.op("pe", lambda e, fv=fv, nn=nn, ec=ec:
                                     e.matmul(P[2 + nn][:, :],
                                              Wv[fv % 2][:, ec * 128:(ec + 1) * 128],
                                              xn[ec][:, nn * 512:(nn + 1) * 512],
                                              start=(ec == 0), stop=(ec == 15)),
                                     r=(wslot, f"xn{ec}"),
                                     w=((pn,) if ec in (0, 15) else ()),
                                     inc=(MS["pe"] if ec == 15 else None))
                            p.op("act", lambda e, nn=nn:
                                 e.activation(out=vst[:, :], in_=P[2 + nn][:, :],
                                              func=AF.Copy),
                                 r=(pn,), w=("a_vst",))
                            p.dma("po", s_v[fv * 128:(fv + 1) * 128,
                                            ts * 1024 + nn * 512:
                                            ts * 1024 + (nn + 1) * 512],
                                  vst[:, :], r=("a_vst",), w=("s_v",), sem="d_a_vst")
            p.barrier()

            with ExitStack() as sx:
                ssb = lambda nm, sh, dt: sx.enter_context(nc.sbuf_tensor(nm, sh, dt))
                acc = ssb("st_acc", [128, 8], F32)
                srow = ssb("st_srow", [128, 1], F32)
                As = [ssb(f"st_As{i}", [128, T], F32) for i in range(2)]
                kb = [ssb(f"st_kb{i}", [128, T], BF16) for i in range(4)]
                qb = [ssb(f"st_qb{i}", [128, 256], BF16) for i in range(4)]
                ats = [ssb(f"st_ats{i}", [128, 512], BF16) for i in range(3)]
                for pr in range(8):
                    eoA, eoB = 2 * pr, 2 * pr + 1
                    for sc in range(16):
                        kslot, qslot = f"st_kb{sc % 4}", f"st_qb{sc % 4}"
                        p.dma("sp", kb[sc % 4][:, :],
                              s_k[sc * 128:(sc + 1) * 128, :], w=(kslot,))
                        qsrc = s_q[(sc * 16 + eoA) * 128:(sc * 16 + eoB + 1) * 128, :]
                        p.dma("sp",
                              qb[sc % 4][:, :].rearrange("pp (b c) -> pp b c", b=2),
                              qsrc.rearrange("(b pp) c -> pp b c", pp=128),
                              w=(qslot,))
                        for half in range(2):
                            for fc in range(4):
                                bank = half * 4 + fc
                                p.op("pe", lambda e, sc=sc, fc=fc, half=half,
                                     bank=bank:
                                     e.matmul(P[bank][:, :],
                                              qb[sc % 4][:, half * 128:
                                                         half * 128 + 128],
                                              kb[sc % 4][:, fc * 512:(fc + 1) * 512],
                                              start=(sc == 0), stop=(sc == 15)),
                                     r=(kslot, qslot),
                                     w=((f"P{bank}",) if sc in (0, 15) else ()),
                                     inc=(MS["pe"] if (sc == 15 or
                                                       (half, fc) == (1, 3))
                                          else None))
                    for half, eo in ((0, eoA), (1, eoB)):
                        for fc in range(4):
                            bank = half * 4 + fc
                            p.op("act", lambda e, half=half, fc=fc, bank=bank:
                                 e.activation(
                                     out=As[half][:, fc * 512:(fc + 1) * 512],
                                     in_=P[bank][:, :], func=AF.Exp, bias=cb[:, 2:3],
                                     accum_out=acc[:, bank:bank + 1]),
                                 r=(f"P{bank}",), w=(f"st_As{half}", "st_acc"))
                        p.op("dve", lambda e, half=half:
                             e.tensor_reduce(out=srow[:, 0:1],
                                             in_=acc[:, half * 4:half * 4 + 4],
                                             axis=AX.X, op=ALU.add),
                             r=("st_acc",), w=("st_srow",))
                        p.op("dve", lambda e, eo=eo:
                             e.reciprocal(out=rcols[:, eo:eo + 1], in_=srow[:, 0:1]),
                             r=("st_srow",), w=("at_rcols",))
                        for b in range(16):
                            bank = half * 4 + b // 4
                            p.op("pe", lambda e, half=half, b=b, bank=bank:
                                 e.transpose(
                                     P[bank][:, (b % 4) * 128:(b % 4 + 1) * 128],
                                     As[half][:, b * 128:(b + 1) * 128],
                                     eye_sb[:, :]),
                                 r=(f"st_As{half}", "eye_sb"),
                                 w=((f"P{bank}",) if b % 4 == 0 else ()),
                                 inc=(MS["pe"] if b % 4 == 3 else None))
                        for b in range(16):
                            bank = half * 4 + b // 4
                            g4 = b // 4
                            aslot = f"st_ats{g4 % 3}"
                            p.op("act", lambda e, b=b, bank=bank, g4=g4:
                                 e.activation(
                                     out=ats[g4 % 3][:, (b % 4) * 128:(b % 4 + 1) * 128],
                                     in_=P[bank][:, (b % 4) * 128:(b % 4 + 1) * 128],
                                     func=AF.Copy),
                                 r=(f"P{bank}",), w=(aslot,))
                            if b % 4 == 3:
                                dst = s_at.rearrange("(b r pp) c -> r pp b c",
                                                     pp=128, r=16)[
                                    eo, :, g4 * 4:g4 * 4 + 4]
                                p.dma("po", dst,
                                      ats[g4 % 3][:, :].rearrange(
                                          "pp (b c) -> pp b c", b=4),
                                      r=(aslot,), w=("s_at",), sem="d_" + aslot)
            p.barrier()

            with ExitStack() as wx:
                wsb = lambda nm, sh, dt: wx.enter_context(nc.sbuf_tensor(nm, sh, dt))
                attsb = [wsb(f"w_att{i}", [128, 512], BF16) for i in range(16)]
                ATb = [wsb(f"w_AT{i}", [128, 1024], BF16) for i in range(4)]
                vt = [wsb(f"w_vt{i}", [128, 512], BF16) for i in range(4)]
                Wob = [wsb(f"w_Wo{i}", [128, 2048], BF16) for i in range(3)]
                for ss in range(4):
                    for eg in range(2):
                        for fc in range(16):
                            aslot, vslot = f"w_AT{fc % 4}", f"w_vt{fc % 4}"
                            strip = s_at[(fc * 16 + eg * 8) * 128:
                                         (fc * 16 + eg * 8 + 8) * 128, :]
                            p.dma("sp",
                                  ATb[fc % 4][:, :].rearrange(
                                      "pp (b c) -> pp b c", b=8),
                                  strip.rearrange("(b pp) c -> pp b c", pp=128),
                                  w=(aslot,))
                            p.dma("act", vt[fc % 4][:, :],
                                  s_v[fc * 128:(fc + 1) * 128,
                                      ss * 512:(ss + 1) * 512], w=(vslot,))
                            for d in range(8):
                                p.op("pe", lambda e, fc=fc, d=d:
                                     e.matmul(P[d][:, :],
                                              ATb[fc % 4][:, d * 128:(d + 1) * 128],
                                              vt[fc % 4][:, :],
                                              start=(fc == 0), stop=(fc == 15)),
                                     r=(aslot, vslot),
                                     w=((f"P{d}",) if fc in (0, 15) else ()),
                                     inc=(MS["pe"] if d == 7 or fc == 15 else None))
                        for d in range(8):
                            eo = eg * 8 + d
                            p.op("dve", lambda e, d=d, eo=eo:
                                 e.tensor_scalar(out=attsb[eo][:, :], in0=P[d][:, :],
                                                 scalar1=rcols[:, eo:eo + 1],
                                                 scalar2=None, op0=ALU.mult),
                                 r=(f"P{d}", "at_rcols"), w=(f"w_att{eo}",))
                    for fo in range(16):
                        wslot = f"w_Wo{fo % 3}"
                        p.dma("sp", Wob[fo % 3][:, :],
                              s_wo2[fo * 128:(fo + 1) * 128, :], w=(wslot,))
                        pn = f"P{fo % 2}"
                        for ec in range(NE):
                            p.op("pe", lambda e, fo=fo, ec=ec:
                                 e.matmul(P[fo % 2][:, :],
                                          Wob[fo % 3][:, ec * 128:(ec + 1) * 128],
                                          attsb[ec][:, :],
                                          start=(ec == 0), stop=(ec == 15)),
                                 r=(wslot, f"w_att{ec}"),
                                 w=((pn,) if ec in (0, 15) else ()),
                                 inc=(MS["pe"] if ec == 15 else None))
                        p.op("dve", lambda e, fo=fo, ss=ss:
                             e.scalar_tensor_tensor(
                                 out=X[fo][:, ss * 512:(ss + 1) * 512],
                                 in0=P[fo % 2][:, :], scalar=1.0,
                                 in1=X[fo][:, ss * 512:(ss + 1) * 512],
                                 op0=ALU.mult, op1=ALU.add),
                             r=(pn,), w=(f"X{fo}",))
            p.barrier()

    if "ffn1" in phases:
        ffn_pass("f1")
    elif "quant" in phases:
        pass
    if want_attn:
        attn()
    if "ffn2" in phases:
        ffn_pass("f2")

    for i in range(NE):
        p.dma("po", yT[i * 128:(i + 1) * 128, :], X[i][:, :], r=(f"X{i}",),
              sem="d_out")
    p.emit("po", lambda e: e.nop(), [("d_out", p.cnt["d_out"])], None, 1)

    sems = {s: es.enter_context(nc.semaphore(s)) for s in list(SEMS) + p.dsems}

    def replay(name):
        def run(eng):
            for fn, waits, inc, n in p.streams[name]:
                for s, v in waits:
                    eng.wait_ge(sems[s], v)
                inst = fn(eng)
                if inc is not None:
                    inst.then_inc(sems[inc], n)
        return run

    with nc.Block() as block:
        block.sync(replay("sp"))
        block.tensor(replay("pe"))
        block.scalar(replay("act"))
        block.vector(replay("dve"))
        block.gpsimd(replay("po"))

    es.close()
    build.last_prog = p
    return nc


_BUILD_CACHE = {}


def _get_nc(phases):
    key = tuple(phases)
    if key not in _BUILD_CACHE:
        _BUILD_CACHE[key] = build(phases)
    return _BUILD_CACHE[key]


def kernel(L0_attn_x, L0_attn_Wq, L0_attn_Wk, L0_attn_Wv, L0_attn_Wo,
           L0_ffn_w_gate_up, L0_ffn_w_proj):
    x = np.asarray(L0_attn_x, dtype=np.float32)
    wq = np.ascontiguousarray(np.asarray(L0_attn_Wq, dtype=np.float32))
    wk = np.ascontiguousarray(np.asarray(L0_attn_Wk, dtype=np.float32))
    wv = np.ascontiguousarray(np.asarray(L0_attn_Wv, dtype=np.float32))
    wo = np.ascontiguousarray(np.asarray(L0_attn_Wo, dtype=np.float32))
    wguT = np.ascontiguousarray(np.asarray(L0_ffn_w_gate_up, dtype=np.float32).T)
    wpT = np.ascontiguousarray(np.asarray(L0_ffn_w_proj, dtype=np.float32).T)
    ind16, sel16, eye = _consts_np()

    nc = _get_nc(PHASES)
    in_maps = []
    for i in range(8):
        in_maps.append({
            "xT": np.ascontiguousarray(x[i].T),
            "wq": wq, "wk": wk, "wv": wv, "wo": wo,
            "wguT": wguT, "wpT": wpT,
            "eye_f": eye, "ind16": ind16, "sel16": sel16,
        })
    res = run_bass_kernel_spmd(nc, in_maps, core_ids=list(range(8)))
    out = np.empty((8, T, D), dtype=np.float32)
    for i in range(8):
        out[i] = res.results[i]["yT"].T
    return out
